# version 07
# baseline (speedup 1.0000x reference)
import numpy as np
import ml_dtypes
import concourse.bass as bass
import concourse.mybir as mybir
from concourse.bass_utils import run_bass_kernel_spmd

F32 = mybir.dt.float32
BF16 = mybir.dt.bfloat16
U8 = mybir.dt.uint8
AF = mybir.ActivationFunctionType
ALU = mybir.AluOpType
AX = mybir.AxisListType
BF = ml_dtypes.bfloat16

ENGINES = ("sync", "gpsimd", "tensor", "vector", "scalar")



class Op:
    __slots__ = ("eng", "fn", "dma", "need_inc", "cum", "deps", "pos", "inc_sem")

    def __init__(self, eng, fn, dma, pos):
        self.eng = eng
        self.fn = fn
        self.dma = dma
        self.need_inc = dma
        self.cum = None
        self.deps = []
        self.pos = pos
        self.inc_sem = None


class Sched:
    def __init__(self, eng_sems, dma_sems=None):
        self.eng_sems = eng_sems
        self.dma_sems = dma_sems or {}
        self.dma_cnt = {e: 0 for e in ENGINES}
        self.last_dma = {e: {} for e in ENGINES}
        self.ops = {e: [] for e in ENGINES}
        self.res = {}
        self.fence_ops = []

    def op(self, eng, fn, r=(), w=(), extra=(), dma=False, out_sem=None):
        o = Op(eng, fn, dma, len(self.ops[eng]))
        if dma and out_sem is None and eng in self.dma_sems:
            pool = self.dma_sems[eng]
            i = self.dma_cnt[eng] % len(pool)
            self.dma_cnt[eng] += 1
            o.inc_sem = pool[i]
            self.last_dma[eng][i] = o
        deps = list(extra) + list(self.fence_ops)
        for k in r:
            e = self.res.get(k)
            if e is not None and e[0] is not None:
                deps.append(e[0])
        for k in w:
            e = self.res.get(k)
            if e is not None:
                deps.extend(e[1])
                if e[0] is not None:
                    deps.append(e[0])
        best = {}
        for d in deps:
            if d is None or d is o:
                continue
            if d.eng == "tensor" and eng == "tensor":
                continue
            key = (d.eng, d.dma, id(d.inc_sem))
            b = best.get(key)
            if b is None or d.pos > b.pos:
                best[key] = d
        for d in best.values():
            d.need_inc = True
            o.deps.append(d)
        for k in r:
            self.res.setdefault(k, [None, []])[1].append(o)
        for k in w:
            self.res[k] = [o, []]
        if out_sem is not None:
            o.inc_sem = out_sem
            o.need_inc = True
        self.ops[eng].append(o)
        return o

    def fence(self):
        fo = []
        for e, ops in self.ops.items():
            if not ops:
                continue
            fo.append(ops[-1])
            fo.extend(self.last_dma[e].values())
        self.fence_ops = fo

    def finalize_and_emit(self, block, final_waits=()):
        cum = {}
        for e in ENGINES:
            for o in self.ops[e]:
                sem = o.inc_sem
                if sem is None:
                    sem = self.eng_sems[e]
                o.inc_sem = sem
                if o.need_inc:
                    cum[id(sem)] = cum.get(id(sem), 0) + (16 if o.dma else 1)
                o.cum = cum.get(id(sem), 0)
        stats = {e: (len(self.ops[e]), sum(1 for o in self.ops[e] if o.need_inc))
                 for e in ENGINES}
        for ename in ENGINES:
            def mk(name, olist):
                def _(eng):
                    lw = {}
                    for o in olist:
                        for d in o.deps:
                            if d.cum <= lw.get(id(d.inc_sem), 0):
                                continue
                            eng.wait_ge(d.inc_sem, d.cum)
                            lw[id(d.inc_sem)] = d.cum
                        inst = o.fn(eng)
                        if o.need_inc:
                            inst.then_inc(o.inc_sem, 16 if o.dma else 1)
                    if name == "sync":
                        for (sem, val) in final_waits:
                            eng.wait_ge(sem, val)
                return _
            getattr(block, ename)(mk(ename, self.ops[ename]))
        return stats


class View:
    __slots__ = ("ap", "off", "cols")

    def __init__(self, ap, off, cols):
        self.ap = ap
        self.off = off
        self.cols = cols

    def __getitem__(self, idx):
        p, c = idx
        if isinstance(c, slice):
            c0 = 0 if c.start is None else c.start
            c1 = self.cols if c.stop is None else c.stop
        else:
            c0, c1 = c, c + 1
        assert 0 <= c0 < c1 <= self.cols, f"view col range {c0}:{c1} of {self.cols}"
        return self.ap[p, c0:c1]


class Arena:

    def __init__(self, nc, handle, total):
        self.nc = nc
        self.base = nc.lookup_mloc(handle).addr
        self.total = total
        self.off = 0
        self.stack = []
        self.n = 0
        self.peak = 0

    def t(self, dtype, cols, tag=""):
        nbytes = cols * (4 if dtype == F32 else 2)
        nbytes = (nbytes + 31) & ~31
        assert self.off + nbytes <= self.total, \
            f"SBUF arena overflow at {tag}: {self.off + nbytes} > {self.total}"
        name = f"ar{self.n}_{tag}"
        self.n += 1
        hnd = self.nc.virtual_sbuf_tensor(name, [128, cols], dtype)
        self.nc._tensor(hnd.name, list(hnd.shape), hnd.dtype, type="SB")
        mloc = self.nc.lookup_mloc(hnd)
        mloc.allocated = True
        mloc.addr = self.base + self.off
        self.off += nbytes
        self.peak = max(self.peak, self.off)
        return View(hnd.ap(), 0, cols)

    def mark(self):
        self.stack.append(self.off)

    def release(self):
        self.off = self.stack.pop()



SEQ = 2048
D = 1024
NT = 16
NCH = 8
CH = 8
NH = 16
HD = 64
NE = 8
H4 = 4096
LR = 0.02
DEBUG_SPILLS = False


def build_program():
    nc = bass.Bass("TRN2", target_bir_lowering=False)

    def din(name, shape, dt=F32):
        return nc.dram_tensor(name, shape, dt, kind="ExternalInput").ap()

    d_x = din("x", [2 * SEQ, D])
    d_wbt = din("wbt", [D, D])
    d_wpt = din("wpt", [D, D])
    d_wpraw = din("wpraw", [D, D])
    d_wqt = din("wqt", [D, D])
    d_wkt = din("wkt", [D, D])
    d_wvt = din("wvt", [D, D])
    d_wot = din("wot", [D, D])
    d_w1d = din("w1d", [NE * 32 * 128, 1024])
    d_w2d = din("w2d", [NE * 2 * 32 * 128, 512])
    d_b1t = din("b1t", [128, 256])
    d_b2 = din("b2", [NE, D])
    d_gatet = din("gatet", [D, 8])
    d_initac = din("initac", [D, 16])
    d_initbtc = din("initbtc", [D, 16])
    d_initat = din("initat", [16, D])
    d_initb = din("initb", [16, D])
    d_lng = din("lng", [1, D])
    d_lnb = din("lnb", [1, D])
    d_moew = din("moew", [1, D])
    d_attnwc = din("attnwc", [128, 8])
    d_identf = din("identf", [128, 128])
    d_tri = din("tri", [128, 128])
    d_onesf = din("onesf", [1, 128])
    d_onesb = din("onesb", [1, 128], BF16)
    d_neg2 = din("neg2", [1, 32])
    d_r2t = din("r2t", [128, 128], BF16)
    d_cosb = din("cosb", [128, SEQ], BF16)
    d_sinb = din("sinb", [128, SEQ], BF16)

    d_out = nc.dram_tensor("out", [2 * SEQ, D], F32, kind="ExternalOutput").ap()
    kd = "ExternalOutput" if DEBUG_SPILLS else "Internal"
    d_r1sp = nc.dram_tensor("r1sp", [1, SEQ], F32, kind=kd).ap()
    d_xnsp = nc.dram_tensor("xnsp", [SEQ, D], F32, kind=kd).ap()
    d_xtmsp = nc.dram_tensor("xtmsp", [SEQ, D], BF16, kind=kd).ap()
    d_ssp = nc.dram_tensor("ssp", [SEQ, D], F32, kind=kd).ap()
    d_hsp = nc.dram_tensor("hsp", [SEQ, D], F32, kind=kd).ap()
    if DEBUG_SPILLS:
        d_dbgtt = nc.dram_tensor("dbgtt", [128, 16384], BF16, kind=kd).ap()
        d_dbgq = nc.dram_tensor("dbgq", [128, 16384], BF16, kind=kd).ap()
        d_dbgk = nc.dram_tensor("dbgk", [128, 16384], BF16, kind=kd).ap()
        d_dbgv = nc.dram_tensor("dbgv", [128, 16384], BF16, kind=kd).ap()
        d_dbgao = nc.dram_tensor("dbgao", [128, 16384], BF16, kind=kd).ap()
        d_dbgtn = nc.dram_tensor("dbgtn", [128, 8192], BF16, kind=kd).ap()
        d_dbggw = nc.dram_tensor("dbggw", [128, 64], F32, kind=kd).ap()
        d_dbgmoe = nc.dram_tensor("dbgmoe", [1024, D], F32, kind=kd).ap()
        d_dbghid = nc.dram_tensor("dbghid", [128, 32768], BF16, kind=kd).ap()

    ARENA_BYTES = 181248

    import contextlib
    with contextlib.ExitStack() as es:
        ARH = es.enter_context(nc.sbuf_tensor("arena", [128, ARENA_BYTES], U8))
        PS = [es.enter_context(nc.psum_tensor(f"ps{i}", [128, 512], F32))
              for i in range(8)]
        sSY = es.enter_context(nc.semaphore("sSY"))
        sGP = es.enter_context(nc.semaphore("sGP"))
        sPE = es.enter_context(nc.semaphore("sPE"))
        sDV = es.enter_context(nc.semaphore("sDV"))
        sAC = es.enter_context(nc.semaphore("sAC"))
        sOUT = es.enter_context(nc.semaphore("sOUT"))
        block = es.enter_context(nc.Block())
        A = Arena(nc, ARH, ARENA_BYTES)
        NPOOL = 8
        pool_sy = [es.enter_context(nc.semaphore(f"dsy{i}")) for i in range(NPOOL)]
        pool_gp = [es.enter_context(nc.semaphore(f"dgp{i}")) for i in range(NPOOL)]
        pool_st = [es.enter_context(nc.semaphore(f"dst{i}")) for i in range(NPOOL)]
        S = Sched({"sync": sSY, "gpsimd": sGP, "tensor": sPE,
                   "vector": sDV, "scalar": sAC},
                  dma_sems={"sync": pool_sy, "gpsimd": pool_gp,
                            "scalar": pool_st})

        def dma(eng, dst, src, r=(), w=(), out_sem=None):
            return S.op(eng, lambda e, d=dst, s=src: e.dma_start(out=d, in_=s),
                        r=r, w=w, dma=True, out_sem=out_sem)

        def mm(out, lhsT, rhs, start, stop, r=(), w=(), tp=None):
            return S.op("tensor",
                        lambda e, o=out, l=lhsT, rr=rhs, st=start, sp=stop, t=tp:
                        e.matmul(o, l, rr, start=st, stop=sp, tile_position=t),
                        r=r, w=w)

        def trans(out, in_, ident, r=(), w=()):
            return S.op("tensor",
                        lambda e, o=out, i=in_, d=ident: e.transpose(o, i, d),
                        r=r, w=w)

        def vcp(out, in_, r=(), w=()):
            return S.op("vector", lambda e, o=out, i=in_: e.tensor_copy(o, i),
                        r=r, w=w)

        def vtt(op, out, in0, in1, r=(), w=()):
            return S.op("vector",
                        lambda e, o=out, a=in0, b=in1, p=op: e.tensor_tensor(o, a, b, p),
                        r=r, w=w)

        def vts(out, in0, s1, s2, op0, op1=None, r=(), w=()):
            if op1 is None:
                return S.op("vector",
                            lambda e, o=out, a=in0, x=s1, p=op0:
                            e.tensor_scalar(o, a, x, None, p),
                            r=r, w=w)
            return S.op("vector",
                        lambda e, o=out, a=in0, x=s1, y=s2, p=op0, q=op1:
                        e.tensor_scalar(o, a, x, y, p, q),
                        r=r, w=w)

        def vstt(out, in0, scal, in1, op0, op1, r=(), w=()):
            return S.op("vector",
                        lambda e, o=out, a=in0, x=scal, b=in1, p=op0, q=op1:
                        e.scalar_tensor_tensor(o, a, x, b, p, q),
                        r=r, w=w)

        def act(out, in_, func, bias=None, scale=None, accum=None, r=(), w=()):
            def f(e, o=out, i=in_, fn=func, b=bias, sc=scale, a=accum):
                kw = {}
                if b is not None:
                    kw["bias"] = b
                if sc is not None:
                    kw["scale"] = sc
                if a is not None:
                    kw["accum_out"] = a
                return e.activation(o, i, fn, **kw)
            return S.op("scalar", f, r=r, w=w)

        def rmax(out, in_, r=(), w=()):
            return S.op("vector", lambda e, o=out, i=in_: e.reduce_max(o, i, axis=AX.X),
                        r=r, w=w)

        def rsum(out, in_, r=(), w=()):
            return S.op("vector", lambda e, o=out, i=in_: e.reduce_sum(o, i, axis=AX.X),
                        r=r, w=w)

        def vrec(out, in_, r=(), w=()):
            return S.op("vector", lambda e, o=out, i=in_: e.reciprocal(o, i), r=r, w=w)

        def vmemset(ap_, val, r=(), w=()):
            return S.op("vector", lambda e, a=ap_, v=val: e.memset(a, v), r=r, w=w)

        def vbn(out, in_, r=(), w=()):
            return S.op("vector", lambda e, o=out, i=in_: e.bn_stats(o, i), r=r, w=w)

        def vbnagg(out, in_, r=(), w=()):
            return S.op("vector", lambda e, o=out, i=in_: e.bn_aggr(o, i), r=r, w=w)

        def vttr(out, in0, in1, accum, r=(), w=()):
            return S.op("vector",
                        lambda e, o=out, a=in0, b=in1, ac=accum:
                        e.tensor_tensor_reduce(o, a, b, 1.0, 0.0, ALU.mult,
                                               ALU.add, ac),
                        r=r, w=w)

        IDF = A.t(F32, 128, "idf")
        TRI = A.t(F32, 128, "tri")
        ONESF = A.t(F32, 128, "onesf")
        ONESB = A.t(BF16, 128, "onesb")
        NEG2 = A.t(F32, 32, "neg2")
        ATTNW = A.t(F32, 16, "attnw")
        B1T = A.t(F32, 256, "b1t")
        R2T = A.t(BF16, 128, "r2t")
        GATET = A.t(BF16, 64, "gatet")
        WPTA = A.t(BF16, 128, "wpta")
        INITBT = A.t(BF16, 128, "initbt")

        dma("sync", IDF[:, :], d_identf, w=[("idf",)])
        dma("sync", TRI[:, :], d_tri, w=[("tri",)])
        dma("sync", ONESF[0:1, :], d_onesf, w=[("onesf",)])
        dma("sync", ONESB[0:1, :], d_onesb, w=[("onesb",)])
        dma("sync", NEG2[0:1, :], d_neg2, w=[("neg2",)])
        dma("sync", ATTNW[:, 0:8], d_attnwc, w=[("attnw",)])
        dma("sync", B1T[:, :], d_b1t, w=[("b1t",)])
        dma("sync", R2T[:, :], d_r2t, w=[("r2t",)])
        for c in range(NCH):
            dma("gpsimd", GATET[:, c * 8:(c + 1) * 8],
                d_gatet[c * 128:(c + 1) * 128, :], w=[("gatet",)])
            dma("gpsimd", INITBT[:, c * 16:(c + 1) * 16],
                d_initbtc[c * 128:(c + 1) * 128, :], w=[("initbt",)])

        A.mark()
        WPR = A.t(BF16, 8192, "wpr")
        INA = A.t(BF16, 128, "ina")
        for c in range(NCH):
            dma("gpsimd", WPR[:, c * 1024:(c + 1) * 1024],
                d_wpraw[c * 128:(c + 1) * 128, :], w=[("wpr", c)])
            dma("gpsimd", INA[:, c * 16:(c + 1) * 16],
                d_initac[c * 128:(c + 1) * 128, :], w=[("ina",)])
        for m in range(8):
            for hc in range(NCH):
                mm(PS[0][:, 0:16], WPR[:, hc * 1024 + m * 128: hc * 1024 + (m + 1) * 128],
                   INA[:, hc * 16:(hc + 1) * 16], hc == 0, hc == 7,
                   r=[("wpr", hc), ("ina",)], w=[("ps", 0)])
            vcp(WPTA[:, m * 16:(m + 1) * 16], PS[0][:, 0:16],
                r=[("ps", 0)], w=[("wpta",)])
        S.fence()
        A.release()

        n_out = 0

        for s in range(2):
            A.mark()
            XNT = A.t(BF16, 16384, "xnt")

            A.mark()
            SHT = A.t(BF16, 16384, "sht")
            GAMB = A.t(F32, D, "gamb")
            BETB = A.t(F32, D, "betb")
            GROW = A.t(F32, D, "grow")
            XST = A.t(F32, 2048, "xst")
            XNS = A.t(F32, 2048, "xns")
            X2A = A.t(F32, D, "x2a")
            X2B = A.t(F32, D, "x2b")
            XTMC = A.t(BF16, 2048, "xtmc")
            BN = A.t(F32, 16, "bn")
            MV = A.t(F32, 8, "mv")
            SMALL = A.t(F32, 32, "small")
            RSTD1C = A.t(F32, 16, "r1c")
            R1ROW = A.t(F32, SEQ, "r1row")

            for (drow, dst, tag) in ((d_lng, GAMB, "gamb"), (d_lnb, BETB, "betb")):
                dma("sync", GROW[0:1, :], drow, r=[("grow",)], w=[("grow",)])
                for nb in range(2):
                    mm(PS[0][:, 0:512], ONESF[0:1, :],
                       GROW[0:1, nb * 512:(nb + 1) * 512], True, True,
                       r=[("onesf",), ("grow",)], w=[("ps", 0)])
                    vcp(dst[:, nb * 512:(nb + 1) * 512], PS[0][:, 0:512],
                        r=[("ps", 0)], w=[(tag,)])

            for c in range(NCH):
                vmemset(SHT[:, c * 2048 + 2047: c * 2048 + 2048], 0.0,
                        w=[("sht", c, 16)])

            for j in range(NT):
                sl = j % 2
                xv = XST[:, sl * 1024:(sl + 1) * 1024]
                dma("sync", xv, d_x[s * SEQ + j * 128: s * SEQ + (j + 1) * 128, :],
                    w=[("xst", sl)])
                vbn(BN[:, 0:6], XST[:, sl * 1024: sl * 1024 + 512], r=[("xst", sl)], w=[("bn",)])
                vbn(BN[:, 6:12], XST[:, sl * 1024 + 512: (sl + 1) * 1024], r=[("xst", sl)], w=[("bn",)])
                vbnagg(MV[:, 0:2], BN[:, 0:12], r=[("bn",)], w=[("mv",)])
                vts(SMALL[:, 8:9], MV[:, 1:2], 1e-5, None, ALU.add, r=[("mv",)], w=[("sq1a",)])
                act(SMALL[:, 0:1], SMALL[:, 8:9], AF.Sqrt, r=[("sq1a",)], w=[("sq1",)])
                vrec(SMALL[:, 1:2], SMALL[:, 0:1], r=[("sq1",)], w=[("rstd",)])
                xnv = XNS[:, sl * 1024:(sl + 1) * 1024]
                vts(xnv, xv, MV[:, 0:1], SMALL[:, 1:2], ALU.subtract, ALU.mult,
                    r=[("xst", sl), ("mv",), ("rstd",)], w=[("xns", sl)])
                dma("scalar", d_xnsp[j * 128:(j + 1) * 128, :], xnv,
                    r=[("xns", sl)], w=[("xnsp", j)])
                xtv = XTMC[:, sl * 1024:(sl + 1) * 1024]
                vcp(xtv, xnv, r=[("xns", sl)], w=[("xtmc", sl)])
                dma("scalar", d_xtmsp[j * 128:(j + 1) * 128, :], xtv,
                    r=[("xtmc", sl)], w=[("xtmsp", j)])
                for c in range(NCH):
                    b = 2 + (c % 2)
                    trans(PS[b][:, 0:128], XNS[:, sl * 1024 + c * 128: sl * 1024 + (c + 1) * 128],
                          IDF[:, :], r=[("xns", sl), ("idf",)], w=[("ps", b)])
                    vcp(XNT[:, c * 2048 + j * 128: c * 2048 + (j + 1) * 128],
                        PS[b][:, 0:128], r=[("ps", b)], w=[("xnt", c, j)])
                vbn(BN[:, 0:6], XNS[:, sl * 1024: sl * 1024 + 512], r=[("xns", sl)], w=[("bn",)])
                vbn(BN[:, 6:12], XNS[:, sl * 1024 + 512: (sl + 1) * 1024], r=[("xns", sl)], w=[("bn",)])
                vbnagg(MV[:, 2:4], BN[:, 0:12], r=[("bn",)], w=[("mv2",)])
                vts(SMALL[:, 9:10], MV[:, 3:4], 1e-5, None, ALU.add, r=[("mv2",)], w=[("sq2a",)])
                act(SMALL[:, 2:3], SMALL[:, 9:10], AF.Sqrt, r=[("sq2a",)], w=[("sq2",)])
                vrec(SMALL[:, 3:4], SMALL[:, 2:3], r=[("sq2",)], w=[("rstd2",)])
                vtt(ALU.mult, SMALL[:, 4:5], MV[:, 2:3], MV[:, 2:3], r=[("mv2",)], w=[("m2sq",)])
                vtt(ALU.add, SMALL[:, 5:6], SMALL[:, 4:5], MV[:, 3:4],
                    r=[("m2sq",), ("mv2",)], w=[("msq",)])
                vts(SMALL[:, 10:11], SMALL[:, 5:6], 1e-6, None, ALU.add, r=[("msq",)], w=[("sq3a",)])
                act(SMALL[:, 6:7], SMALL[:, 10:11], AF.Sqrt, r=[("sq3a",)], w=[("sq3",)])
                vrec(RSTD1C[:, j: j + 1], SMALL[:, 6:7], r=[("sq3",)], w=[("r1c", j)])
                vts(X2A[:, 0:1024], xnv, MV[:, 2:3], SMALL[:, 3:4], ALU.subtract, ALU.mult,
                    r=[("xns", sl), ("mv2",), ("rstd2",)], w=[("x2a",)])
                vtt(ALU.mult, X2B[:, 0:1024], X2A[:, 0:1024], GAMB[:, 0:1024],
                    r=[("x2a",), ("gamb",)], w=[("x2b",)])
                vtt(ALU.add, X2B[:, 0:1024], X2B[:, 0:1024], BETB[:, 0:1024],
                    r=[("x2b",), ("betb",)], w=[("x2b",)])
                for c in range(NCH):
                    b = 2 + (c % 2)
                    trans(PS[b][:, 0:128], X2B[:, c * 128:(c + 1) * 128],
                          IDF[:, :], r=[("x2b",), ("idf",)], w=[("ps", b)])
                    if j == 0:
                        vcp(SHT[:, c * 2048 + 0: c * 2048 + 127],
                            PS[b][:, 1:128], r=[("ps", b)], w=[("sht", c, 0)])
                    else:
                        vcp(SHT[:, c * 2048 + j * 128 - 1: c * 2048 + j * 128 + 127],
                            PS[b][:, 0:128], r=[("ps", b)], w=[("sht", c, j)])

            for j in range(NT):
                trans(PS[4][0:1, 0:128], RSTD1C[:, j: j + 1], IDF[:, :],
                      r=[("r1c", j), ("idf",)], w=[("ps", 4)])
                vcp(R1ROW[0:1, j * 128:(j + 1) * 128], PS[4][0:1, 0:128],
                    r=[("ps", 4)], w=[("r1row",)])
            dma("scalar", d_r1sp, R1ROW[0:1, 0:SEQ], r=[("r1row",)], w=[("r1sp",)])

            S.fence()
            A.release()

            A.mark()
            SHTb = A.t(BF16, 16384, "sht2")
            WBT = A.t(BF16, 8192, "wbtv")
            WPT = A.t(BF16, 8192, "wptv")
            VCH = A.t(BF16, 4096, "vch")
            XTM = A.t(BF16, 4096, "xtm")
            SST = A.t(F32, 2048, "sst")
            PIN = A.t(BF16, 32, "pin")
            PERR = A.t(BF16, 32, "perr")
            MIDT = A.t(BF16, 256, "midt")
            AEF = A.t(BF16, D, "aeff")
            BEFFT = A.t(BF16, 128, "befft")
            IAT = A.t(F32, D, "iat")
            IAT2 = A.t(F32, D, "iat2")
            IB = A.t(F32, D, "ib")
            ACCA = A.t(F32, D, "acca")
            ACCB = A.t(F32, D, "accb")
            TSCR = A.t(F32, D, "tscr")
            BEFF = A.t(F32, D, "beff")
            SM = A.t(F32, 48, "sm")

            for c in range(NCH):
                dma("gpsimd", WBT[:, c * 1024:(c + 1) * 1024],
                    d_wbt[c * 128:(c + 1) * 128, :], w=[("wbt", c)])
                dma("gpsimd", WPT[:, c * 1024:(c + 1) * 1024],
                    d_wpt[c * 128:(c + 1) * 128, :], w=[("wpt", c)])
            dma("sync", IAT[0:16, :], d_initat, w=[("iat",)])
            dma("sync", IB[0:16, :], d_initb, w=[("ib",)])
            vts(IAT2[0:16, :], IAT[0:16, :], 2.0, None, ALU.mult,
                r=[("iat",)], w=[("iat2",)])
            vmemset(ACCA[0:16, :], 0.0, w=[("acca",)])
            vmemset(ACCB[0:16, :], 0.0, w=[("accb",)])

            for c in range(CH):
                vs = c % 2
                for tt in range(2):
                    jt = 2 * c + tt
                    dma("sync", XTM[:, (vs * 2 + tt) * 1024:(vs * 2 + tt + 1) * 1024],
                        d_xtmsp[jt * 128:(jt + 1) * 128, :],
                        r=[("xtmsp", jt)], w=[("xtm", vs, tt)])
                for tt in range(2):
                    jt = 2 * c + tt
                    for blk in range(2):
                        b = (tt * 2 + blk) % 2
                        for cc in range(NCH):
                            mm(PS[b][:, 0:512],
                               SHTb[:, cc * 2048 + jt * 128: cc * 2048 + (jt + 1) * 128],
                               WPT[:, cc * 1024 + blk * 512: cc * 1024 + (blk + 1) * 512],
                               cc == 0, cc == 7,
                               r=[("sht", cc, jt), ("sht", cc, jt + 1), ("wpt", cc)],
                               w=[("ps", b)])
                        vcp(VCH[:, (vs * 2 + tt) * 1024 + blk * 512:
                                (vs * 2 + tt) * 1024 + (blk + 1) * 512],
                            PS[b][:, 0:512], r=[("ps", b)], w=[("vch", vs, tt, blk)])

                vtt(ALU.mult, TSCR[0:16, :], ACCA[0:16, :], ACCA[0:16, :],
                    r=[("acca",)], w=[("tscr",)])
                rsum(SM[0:16, 0:1], TSCR[0:16, :], r=[("tscr",)], w=[("sa",)])
                vtt(ALU.mult, TSCR[0:16, :], ACCB[0:16, :], ACCB[0:16, :],
                    r=[("accb",), ("sa",)], w=[("tscr",)])
                rsum(SM[0:16, 1:2], TSCR[0:16, :], r=[("tscr",)], w=[("sb",)])
                vtt(ALU.add, SM[0:16, 2:3], SM[0:16, 0:1], SM[0:16, 1:2],
                    r=[("sa",), ("sb",)], w=[("sab",)])
                trans(PS[7][0:1, 0:16], SM[0:16, 2:3], IDF[0:16, 0:16],
                      r=[("sab",), ("idf",)], w=[("ps", 7)])
                vcp(SM[0:1, 16:32], PS[7][0:1, 0:16], r=[("ps", 7)], w=[("row16",)])
                rsum(SM[0:1, 8:9], SM[0:1, 16:32], r=[("row16",)], w=[("ssq",)])
                act(SM[0:1, 9:10], SM[0:1, 8:9], AF.Sqrt, scale=LR * LR,
                    r=[("ssq",)], w=[("tn",)])
                vts(SM[0:1, 10:11], SM[0:1, 9:10], 1e-6, None, ALU.add,
                    r=[("tn",)], w=[("tne",)])
                vrec(SM[0:1, 12:13], SM[0:1, 10:11], r=[("tne",)], w=[("rec",)])
                vts(SM[0:1, 11:12], SM[0:1, 12:13], 1.0, None, ALU.min,
                    r=[("rec",)], w=[("coef",)])
                mm(PS[7][0:16, 0:1], NEG2[0:1, 0:16], SM[0:1, 11:12], True, True,
                   r=[("neg2",), ("coef",)], w=[("ps", 7)])
                vcp(SM[0:16, 3:4], PS[7][0:16, 0:1], r=[("ps", 7)], w=[("c2a",)])
                mm(PS[7][0:16, 0:1], NEG2[0:1, 16:32], SM[0:1, 11:12], True, True,
                   r=[("neg2",), ("coef",)], w=[("ps", 7)])
                vcp(SM[0:16, 4:5], PS[7][0:16, 0:1], r=[("ps", 7)], w=[("c2b",)])
                vstt(AEF[0:16, :], ACCA[0:16, :], SM[0:16, 3:4], IAT2[0:16, :],
                     ALU.mult, ALU.add, r=[("acca",), ("c2a",), ("iat2",)], w=[("aef",)])
                vstt(BEFF[0:16, :], ACCB[0:16, :], SM[0:16, 4:5], IB[0:16, :],
                     ALU.mult, ALU.add, r=[("accb",), ("c2b",), ("ib",)], w=[("beff",)])
                for cc in range(NCH):
                    trans(PS[7][:, 0:16], BEFF[0:16, cc * 128:(cc + 1) * 128],
                          IDF[0:16, 0:16], r=[("beff",), ("idf",)], w=[("ps", 7)])
                    vcp(BEFFT[:, cc * 16:(cc + 1) * 16], PS[7][:, 0:16],
                        r=[("ps", 7)], w=[("befft", cc)])

                for tt in range(2):
                    jt = 2 * c + tt
                    for cc in range(NCH):
                        mm(PS[4][0:16, 0:128], BEFFT[:, cc * 16:(cc + 1) * 16],
                           XNT[:, cc * 2048 + jt * 128: cc * 2048 + (jt + 1) * 128],
                           cc == 0, cc == 7,
                           r=[("befft", cc), ("xnt", cc, jt)], w=[("ps", 4)])
                    vcp(MIDT[0:16, tt * 128:(tt + 1) * 128], PS[4][0:16, 0:128],
                        r=[("ps", 4)], w=[("midt", tt)])
                for tt in range(2):
                    jt = 2 * c + tt
                    ss = jt % 2
                    for blk in range(2):
                        b = 5 + blk
                        for cc in range(NCH):
                            mm(PS[b][:, 0:512],
                               XNT[:, cc * 2048 + jt * 128: cc * 2048 + (jt + 1) * 128],
                               WBT[:, cc * 1024 + blk * 512: cc * 1024 + (blk + 1) * 512],
                               cc == 0, False,
                               r=[("xnt", cc, jt), ("wbt", cc)], w=[("ps", b)])
                        mm(PS[b][:, 0:512], MIDT[0:16, tt * 128:(tt + 1) * 128],
                           AEF[0:16, blk * 512:(blk + 1) * 512], False, True,
                           r=[("midt", tt), ("aef",)], w=[("ps", b)])
                        vcp(SST[:, ss * 1024 + blk * 512: ss * 1024 + (blk + 1) * 512],
                            PS[b][:, 0:512], r=[("ps", b)], w=[("sst", ss, blk)])
                    dma("scalar", d_ssp[jt * 128:(jt + 1) * 128, :],
                        SST[:, ss * 1024:(ss + 1) * 1024],
                        r=[("sst", ss, 0), ("sst", ss, 1)], w=[("ssp", jt)])

                for tt in range(2):
                    jt = 2 * c + tt
                    b = tt % 2
                    for cc in range(NCH):
                        mm(PS[b][:, 0:16],
                           XNT[:, cc * 2048 + jt * 128: cc * 2048 + (jt + 1) * 128],
                           INITBT[:, cc * 16:(cc + 1) * 16], cc == 0, cc == 7,
                           r=[("xnt", cc, jt), ("initbt",)], w=[("ps", b)])
                    vcp(PIN[:, tt * 16:(tt + 1) * 16], PS[b][:, 0:16],
                        r=[("ps", b)], w=[("pin", tt)])
                    for cc in range(NCH):
                        mm(PS[2 + b][:, 0:16],
                           SHTb[:, cc * 2048 + jt * 128: cc * 2048 + (jt + 1) * 128],
                           WPTA[:, cc * 16:(cc + 1) * 16], cc == 0, cc == 7,
                           r=[("sht", cc, jt), ("sht", cc, jt + 1), ("wpta",)],
                           w=[("ps", 2 + b)])
                    vcp(PERR[:, tt * 16:(tt + 1) * 16], PS[2 + b][:, 0:16],
                        r=[("ps", 2 + b)], w=[("perr", tt)])

                for nb in range(2):
                    b = 2 + nb
                    for tt in range(2):
                        mm(PS[b][0:16, 0:512], PIN[:, tt * 16:(tt + 1) * 16],
                           VCH[:, (vs * 2 + tt) * 1024 + nb * 512:
                               (vs * 2 + tt) * 1024 + (nb + 1) * 512],
                           tt == 0, tt == 1,
                           r=[("pin", tt), ("vch", vs, tt, nb)], w=[("ps", b)])
                    vtt(ALU.add, ACCA[0:16, nb * 512:(nb + 1) * 512],
                        ACCA[0:16, nb * 512:(nb + 1) * 512], PS[b][0:16, 0:512],
                        r=[("acca",), ("ps", b)], w=[("acca",)])
                for nb in range(2):
                    b = 2 + nb
                    for tt in range(2):
                        mm(PS[b][0:16, 0:512], PERR[:, tt * 16:(tt + 1) * 16],
                           XTM[:, (vs * 2 + tt) * 1024 + nb * 512:
                               (vs * 2 + tt) * 1024 + (nb + 1) * 512],
                           tt == 0, tt == 1,
                           r=[("perr", tt), ("xtm", vs, tt)], w=[("ps", b)])
                    vtt(ALU.add, ACCB[0:16, nb * 512:(nb + 1) * 512],
                        ACCB[0:16, nb * 512:(nb + 1) * 512], PS[b][0:16, 0:512],
                        r=[("accb",), ("ps", b)], w=[("accb",)])

            S.fence()
            A.release()

            A.mark()
            A.mark()
            R1B = A.t(F32, SEQ, "r1b")
            R1R2 = A.t(F32, SEQ, "r1r2")
            dma("sync", R1R2[0:1, 0:SEQ], d_r1sp, r=[("r1sp",)], w=[("r1r2",)])
            for nb in range(4):
                mm(PS[5][:, 0:512], ONESF[0:1, :],
                   R1R2[0:1, nb * 512:(nb + 1) * 512], True, True,
                   r=[("onesf",), ("r1r2",)], w=[("ps", 5)])
                vcp(R1B[:, nb * 512:(nb + 1) * 512], PS[5][:, 0:512],
                    r=[("ps", 5)], w=[("r1b",)])
            for c in range(NCH):
                vstt(XNT[:, c * 2048:(c + 1) * 2048], XNT[:, c * 2048:(c + 1) * 2048],
                     ATTNW[:, c: c + 1], R1B[:, 0:SEQ], ALU.mult, ALU.mult,
                     r=[("xnt", c, j) for j in range(NT)] + [("attnw",), ("r1b",)],
                     w=[("tt", c)])
            S.fence()
            A.release()

            QT = A.t(BF16, 16384, "qt")
            KT = A.t(BF16, 16384, "kt")
            VTOK = A.t(BF16, 16384, "vtok")
            A.mark()
            WSTR = A.t(BF16, 8192, "wstr")
            QRAW = A.t(BF16, 1024, "qraw")
            RT1 = A.t(F32, 1024, "rt1")
            RT2 = A.t(F32, 1024, "rt2")
            COSB = A.t(BF16, SEQ, "cosb")
            SINB = A.t(BF16, SEQ, "sinb")
            dma("sync", COSB[:, :], d_cosb, w=[("cosb",)])
            dma("sync", SINB[:, :], d_sinb, w=[("sinb",)])

            for (dw, dstv, tag) in ((d_wqt, QT, "qt"), (d_wkt, KT, "kt")):
                for cc in range(NCH):
                    dma("gpsimd", WSTR[:, cc * 1024:(cc + 1) * 1024],
                        dw[cc * 128:(cc + 1) * 128, :], w=[("wstr", cc)])
                for m in range(NCH):
                    for tb in range(4):
                        bq = m % 2
                        for cc in range(NCH):
                            mm(PS[bq][:, 0:512],
                               WSTR[:, cc * 1024 + m * 128: cc * 1024 + (m + 1) * 128],
                               XNT[:, cc * 2048 + tb * 512: cc * 2048 + (tb + 1) * 512],
                               cc == 0, cc == 7,
                               r=[("wstr", cc), ("tt", cc)], w=[("ps", bq)])
                        qs = (m * 4 + tb) % 2
                        vcp(QRAW[:, qs * 512:(qs + 1) * 512], PS[bq][:, 0:512],
                            r=[("ps", bq)], w=[("qraw", qs)])
                        br = 2 + qs
                        mm(PS[br][:, 0:512], R2T[:, 0:128], QRAW[:, qs * 512:(qs + 1) * 512],
                           True, True, r=[("r2t",), ("qraw", qs)], w=[("ps", br)])
                        vtt(ALU.mult, RT1[:, qs * 512:(qs + 1) * 512],
                            QRAW[:, qs * 512:(qs + 1) * 512],
                            COSB[:, tb * 512:(tb + 1) * 512],
                            r=[("qraw", qs), ("cosb",)], w=[("rt1", qs)])
                        vtt(ALU.mult, RT2[:, qs * 512:(qs + 1) * 512],
                            PS[br][:, 0:512], SINB[:, tb * 512:(tb + 1) * 512],
                            r=[("ps", br), ("sinb",)], w=[("rt2", qs)])
                        vtt(ALU.add, dstv[:, m * 2048 + tb * 512: m * 2048 + (tb + 1) * 512],
                            RT1[:, qs * 512:(qs + 1) * 512], RT2[:, qs * 512:(qs + 1) * 512],
                            r=[("rt1", qs), ("rt2", qs)], w=[(tag, m, tb)])
            for cc in range(NCH):
                dma("gpsimd", WSTR[:, cc * 1024:(cc + 1) * 1024],
                    d_wvt[cc * 128:(cc + 1) * 128, :], w=[("wstr", cc)])
            for j in range(NT):
                for blk in range(2):
                    b = (j * 2 + blk) % 2
                    for cc in range(NCH):
                        mm(PS[b][:, 0:512],
                           XNT[:, cc * 2048 + j * 128: cc * 2048 + (j + 1) * 128],
                           WSTR[:, cc * 1024 + blk * 512: cc * 1024 + (blk + 1) * 512],
                           cc == 0, cc == 7,
                           r=[("tt", cc), ("wstr", cc)], w=[("ps", b)])
                    vcp(VTOK[:, j * 1024 + blk * 512: j * 1024 + (blk + 1) * 512],
                        PS[b][:, 0:512], r=[("ps", b)], w=[("vtok", j, blk)])

            if DEBUG_SPILLS and s == 1:
                dma("sync", d_dbgtt, XNT[:, 0:16384],
                    r=[("tt", c) for c in range(NCH)], out_sem=sOUT)
                dma("sync", d_dbgq, QT[:, 0:16384],
                    r=[("qt", m, t4) for m in range(NCH) for t4 in range(4)],
                    out_sem=sOUT)
                dma("sync", d_dbgk, KT[:, 0:16384],
                    r=[("kt", m, t4) for m in range(NCH) for t4 in range(4)],
                    out_sem=sOUT)
                dma("sync", d_dbgv, VTOK[:, 0:16384],
                    r=[("vtok", j, blk) for j in range(NT) for blk in range(2)],
                    out_sem=sOUT)
                n_out += 4

            S.fence()
            A.release()
            AOT = XNT
            A.mark()
            NSL = 4
            SC = A.t(F32, 2048 * NSL, "scb")
            PTS = A.t(BF16, 2048 * 2, "pts")
            ASM = A.t(F32, 16 * NSL, "asm")

            NITER = NT * NH

            def at_meta(it):
                i, h = it // NH, it % NH
                return i, h, (it % NSL) * 2048, it % NSL

            def at_A(it):
                i, h, scv_off, sl = at_meta(it)
                c = h // 2
                pb = (h % 2) * 64
                L = (i + 1) * 128
                nblk = (L + 511) // 512
                for b5 in range(nblk):
                    k0 = b5 * 512
                    wdt = min(512, L - k0)
                    b = (0, 1, 7)[(it * 4 + b5) % 3]
                    ktk = [("kt", c, tbb) for tbb in
                           range(k0 // 512, (k0 + wdt - 1) // 512 + 1)]
                    mm(PS[b][:, 0:wdt],
                       QT[pb:pb + 64, c * 2048 + i * 128: c * 2048 + (i + 1) * 128],
                       KT[pb:pb + 64, c * 2048 + k0: c * 2048 + k0 + wdt],
                       True, True, r=[("qt", c, i // 4)] + ktk, w=[("ps", b)])
                    S.op("scalar", lambda e, o=SC[:, scv_off + k0: scv_off + k0 + wdt],
                         i2=PS[b][:, 0:wdt]: e.copy(o, i2),
                         r=[("ps", b)], w=[("sc", sl)])

            def at_B(it):
                i, h, scv_off, sl = at_meta(it)
                L = (i + 1) * 128
                vtt(ALU.add, SC[:, scv_off + L - 128: scv_off + L],
                    SC[:, scv_off + L - 128: scv_off + L], TRI[:, :],
                    r=[("sc", sl), ("tri",)], w=[("sc", sl)])
                rmax(ASM[:, sl * 8: sl * 8 + 1], SC[:, scv_off: scv_off + L],
                     r=[("sc", sl)], w=[("m1", sl)])
                vts(ASM[:, sl * 8 + 1: sl * 8 + 2], ASM[:, sl * 8: sl * 8 + 1],
                    -0.125, None, ALU.mult, r=[("m1", sl)], w=[("nm", sl)])

            def at_C(it):
                i, h, scv_off, sl = at_meta(it)
                L = (i + 1) * 128
                act(SC[:, scv_off: scv_off + L], SC[:, scv_off: scv_off + L],
                    AF.Exp, bias=ASM[:, sl * 8 + 1: sl * 8 + 2], scale=0.125,
                    accum=ASM[:, sl * 8 + 2: sl * 8 + 3],
                    r=[("sc", sl), ("nm", sl)], w=[("sc", sl), ("rs", sl)])

            def at_D(it):
                i, h, scv_off, sl = at_meta(it)
                c = h // 2
                pb = (h % 2) * 64
                L = (i + 1) * 128
                psl = it % 2
                vrec(ASM[:, sl * 8 + 3: sl * 8 + 4], ASM[:, sl * 8 + 2: sl * 8 + 3],
                     r=[("rs", sl)], w=[("ri", sl)])
                act(SC[:, scv_off: scv_off + L], SC[:, scv_off: scv_off + L],
                    AF.Copy, scale=ASM[:, sl * 8 + 3: sl * 8 + 4],
                    r=[("sc", sl), ("ri", sl)], w=[("sc", sl)])
                ngrp = (i + 1 + 3) // 4
                for g4 in range(ngrp):
                    b = (2, 3, 6)[(it * 4 + g4) % 3]
                    nk = min(4, i + 1 - g4 * 4)
                    for q4 in range(nk):
                        kc = g4 * 4 + q4
                        trans(PS[b][:, q4 * 128:(q4 + 1) * 128],
                              SC[:, scv_off + kc * 128: scv_off + (kc + 1) * 128],
                              IDF[:, :], r=[("sc", sl), ("idf",)], w=[("ps", b)])
                    vcp(PTS[:, psl * 2048 + g4 * 512: psl * 2048 + g4 * 512 + nk * 128],
                        PS[b][:, 0: nk * 128], r=[("ps", b)], w=[("pts", psl, g4)])
                bao = 4 + (h % 2)
                for kc in range(i + 1):
                    mm(PS[bao][pb:pb + 64, 0:128],
                       VTOK[:, kc * 1024 + h * 64: kc * 1024 + h * 64 + 64],
                       PTS[:, psl * 2048 + kc * 128: psl * 2048 + (kc + 1) * 128],
                       kc == 0, kc == i, tp=(0, pb),
                       r=[("vtok", kc, h // 8), ("pts", psl, kc // 4)], w=[("ps", bao)])
                vcp(AOT[pb:pb + 64, c * 2048 + i * 128: c * 2048 + (i + 1) * 128],
                    PS[bao][pb:pb + 64, 0:128], r=[("ps", bao)],
                    w=[("aot", c, i, h % 2)])

            for t in range(NITER + 3):
                if t < NITER:
                    at_A(t)
                if 1 <= t < NITER + 1:
                    at_B(t - 1)
                if 2 <= t < NITER + 2:
                    at_C(t - 2)
                if 3 <= t < NITER + 3:
                    at_D(t - 3)

            if DEBUG_SPILLS and s == 1:
                dma("sync", d_dbgao, AOT[:, 0:16384],
                    r=[("aot", c, i, p) for c in range(NCH) for i in range(NT)
                       for p in range(2)],
                    out_sem=sOUT)
                n_out += 1
            S.fence()
            A.release()
            A.release()

            A.mark()
            WSTR2 = A.t(BF16, 8192, "wstr2")
            XNR = A.t(F32, 2048, "xnr")
            HSTG = A.t(F32, 2048, "hstg")
            for cc in range(NCH):
                dma("gpsimd", WSTR2[:, cc * 1024:(cc + 1) * 1024],
                    d_wot[cc * 128:(cc + 1) * 128, :], w=[("wo", cc)])
            for j in range(NT):
                sl = j % 2
                dma("sync", XNR[:, sl * 1024:(sl + 1) * 1024],
                    d_xnsp[j * 128:(j + 1) * 128, :],
                    r=[("xnsp", j)], w=[("xnr", sl)])
                for blk in range(2):
                    b = (j * 2 + blk) % 2
                    for cc in range(NCH):
                        mm(PS[b][:, 0:512],
                           AOT[:, cc * 2048 + j * 128: cc * 2048 + (j + 1) * 128],
                           WSTR2[:, cc * 1024 + blk * 512: cc * 1024 + (blk + 1) * 512],
                           cc == 0, cc == 7,
                           r=[("aot", cc, j, 0), ("aot", cc, j, 1), ("wo", cc)],
                           w=[("ps", b)])
                    vtt(ALU.add, HSTG[:, sl * 1024 + blk * 512: sl * 1024 + (blk + 1) * 512],
                        PS[b][:, 0:512], XNR[:, sl * 1024 + blk * 512: sl * 1024 + (blk + 1) * 512],
                        r=[("ps", b), ("xnr", sl)], w=[("hstg", sl, blk)])
                dma("scalar", d_hsp[j * 128:(j + 1) * 128, :],
                    HSTG[:, sl * 1024:(sl + 1) * 1024],
                    r=[("hstg", sl, 0), ("hstg", sl, 1)], w=[("hsp", j)])
            S.fence()
            A.release()
            A.release()

            A.mark()
            MOEW = A.t(BF16, D, "moew")
            A.mark()
            GROW2 = A.t(F32, D, "grow2")
            dma("sync", GROW2[0:1, :], d_moew, w=[("grow2",)])
            for nb in range(2):
                mm(PS[0][:, 0:512], ONESF[0:1, :],
                   GROW2[0:1, nb * 512:(nb + 1) * 512], True, True,
                   r=[("onesf",), ("grow2",)], w=[("ps", 0)])
                vcp(MOEW[:, nb * 512:(nb + 1) * 512], PS[0][:, 0:512],
                    r=[("ps", 0)], w=[("moeww",)])
            S.fence()
            A.release()

            HID = A.t(BF16, 32768, "hid")
            TNT = A.t(BF16, 8192, "tnt")
            W1B = A.t(BF16, 4096, "w1b")
            W2B = A.t(BF16, 2048, "w2b")
            B2E = A.t(BF16, 1024, "b2e")
            MOE = A.t(F32, 8192, "moe")
            HTS = A.t(F32, 2048, "hts")
            TNF = A.t(F32, 2048, "tnf")
            LOGW = A.t(F32, 128, "logw")
            GW = A.t(F32, 64, "gw")
            MSM = A.t(F32, 32, "msm")
            SST2 = A.t(F32, 2048, "sst2")
            XST2 = A.t(F32, 2048, "xst2")
            OUTS = A.t(F32, 2048, "outs")

            for tb in range(2):
                j0 = tb * 8
                vmemset(MOE[:, 0:8192], 0.0,
                        w=[("moe", jj, g) for jj in range(8) for g in range(2)])
                def t_T0(jj):
                    j = j0 + jj
                    sl = jj % 2
                    mb = sl * 16
                    lb = sl * 64
                    hv = HTS[:, sl * 1024:(sl + 1) * 1024]
                    dma("sync", hv, d_hsp[j * 128:(j + 1) * 128, :],
                        r=[("hsp", j)], w=[("hts", sl)])
                    vbn(LOGW[:, lb + 40: lb + 46], HTS[:, sl * 1024: sl * 1024 + 512],
                        r=[("hts", sl)], w=[("bn3", sl)])
                    vbn(LOGW[:, lb + 46: lb + 52], HTS[:, sl * 1024 + 512: (sl + 1) * 1024],
                        r=[("hts", sl)], w=[("bn3", sl)])
                    vbnagg(MSM[:, mb: mb + 2], LOGW[:, lb + 40: lb + 52],
                           r=[("bn3", sl)], w=[("mv3", sl)])
                    vtt(ALU.mult, MSM[:, mb + 2: mb + 3], MSM[:, mb: mb + 1],
                        MSM[:, mb: mb + 1], r=[("mv3", sl)], w=[("m3a", sl)])
                    vtt(ALU.add, MSM[:, mb + 3: mb + 4], MSM[:, mb + 2: mb + 3],
                        MSM[:, mb + 1: mb + 2], r=[("m3a", sl), ("mv3", sl)], w=[("m3b", sl)])
                    vts(MSM[:, mb + 4: mb + 5], MSM[:, mb + 3: mb + 4], 1e-6, None,
                        ALU.add, r=[("m3b", sl)], w=[("m3ba", sl)])
                    act(MSM[:, mb + 5: mb + 6], MSM[:, mb + 4: mb + 5], AF.Sqrt,
                        r=[("m3ba", sl)], w=[("m3c", sl)])
                    vrec(MSM[:, mb + 6: mb + 7], MSM[:, mb + 5: mb + 6],
                         r=[("m3c", sl)], w=[("m3d", sl)])

                def t_T1(jj):
                    sl = jj % 2
                    mb = sl * 16
                    hv = HTS[:, sl * 1024:(sl + 1) * 1024]
                    tnfv = TNF[:, sl * 1024:(sl + 1) * 1024]
                    vstt(tnfv, hv, MSM[:, mb + 6: mb + 7], MOEW[:, 0:1024],
                         ALU.mult, ALU.mult,
                         r=[("hts", sl), ("m3d", sl), ("moeww",)], w=[("tnf", sl)])
                    for cc in range(NCH):
                        b = 2 + cc % 2
                        trans(PS[b][:, 0:128], TNF[:, sl * 1024 + cc * 128: sl * 1024 + (cc + 1) * 128],
                              IDF[:, :], r=[("tnf", sl), ("idf",)], w=[("ps", b)])
                        vcp(TNT[:, cc * 1024 + jj * 128: cc * 1024 + (jj + 1) * 128],
                            PS[b][:, 0:128], r=[("ps", b)], w=[("tnt", cc, jj)])

                def t_T2(jj):
                    sl = jj % 2
                    mb = sl * 16
                    lb = sl * 64
                    bg = jj % 2
                    for cc in range(NCH):
                        mm(PS[bg][:, 0:8],
                           TNT[:, cc * 1024 + jj * 128: cc * 1024 + (jj + 1) * 128],
                           GATET[:, cc * 8:(cc + 1) * 8], cc == 0, cc == 7,
                           r=[("tnt", cc, jj), ("gatet",)], w=[("ps", bg)])
                    vcp(LOGW[:, lb: lb + 8], PS[bg][:, 0:8], r=[("ps", bg)], w=[("log", sl)])
                    rmax(MSM[:, mb + 8: mb + 9], LOGW[:, lb: lb + 8],
                         r=[("log", sl)], w=[("g1", sl)])
                    vts(LOGW[:, lb + 8: lb + 16], LOGW[:, lb: lb + 8],
                        MSM[:, mb + 8: mb + 9], None, ALU.is_equal,
                        r=[("log", sl), ("g1", sl)], w=[("mk1", sl)])
                    vstt(LOGW[:, lb + 16: lb + 24], LOGW[:, lb + 8: lb + 16], -1e30,
                         LOGW[:, lb: lb + 8], ALU.mult, ALU.add,
                         r=[("mk1", sl), ("log", sl)], w=[("l2", sl)])
                    rmax(MSM[:, mb + 9: mb + 10], LOGW[:, lb + 16: lb + 24],
                         r=[("l2", sl)], w=[("g2", sl)])
                    vts(LOGW[:, lb + 24: lb + 32], LOGW[:, lb + 16: lb + 24],
                        MSM[:, mb + 9: mb + 10], None, ALU.is_equal,
                        r=[("l2", sl), ("g2", sl)], w=[("mk2", sl)])
                    vts(MSM[:, mb + 10: mb + 11], MSM[:, mb + 8: mb + 9], -1.0, None,
                        ALU.mult, r=[("g1", sl)], w=[("ng1", sl)])
                    act(MSM[:, mb + 11: mb + 12], MSM[:, mb + 9: mb + 10], AF.Exp,
                        bias=MSM[:, mb + 10: mb + 11],
                        r=[("g2", sl), ("ng1", sl)], w=[("e2", sl)])
                    vts(MSM[:, mb + 12: mb + 13], MSM[:, mb + 11: mb + 12], 1.0, None,
                        ALU.add, r=[("e2", sl)], w=[("den", sl)])
                    vrec(MSM[:, mb + 13: mb + 14], MSM[:, mb + 12: mb + 13],
                         r=[("den", sl)], w=[("rd", sl)])
                    vstt(LOGW[:, lb + 32: lb + 40], LOGW[:, lb + 24: lb + 32],
                         MSM[:, mb + 11: mb + 12], LOGW[:, lb + 8: lb + 16],
                         ALU.mult, ALU.add,
                         r=[("mk2", sl), ("e2", sl), ("mk1", sl)], w=[("wn", sl)])
                    vts(GW[:, jj * 8:(jj + 1) * 8], LOGW[:, lb + 32: lb + 40],
                        MSM[:, mb + 13: mb + 14], None, ALU.mult,
                        r=[("wn", sl), ("rd", sl)], w=[("gw", jj)])

                for t_ in range(10):
                    if t_ < 8:
                        t_T0(t_)
                    if 1 <= t_ < 9:
                        t_T1(t_ - 1)
                    if 2 <= t_:
                        t_T2(t_ - 2)

                for e_ in range(NE):
                    for pr in range(16):
                        sb = pr % 2
                        r0 = (e_ * 32 + pr * 2) * 128
                        dma("gpsimd",
                            W1B[:, sb * 2048:(sb + 1) * 2048].rearrange(
                                "p (two d) -> p two d", two=2),
                            d_w1d[r0:r0 + 256, :].rearrange(
                                "(two p) d -> p two d", p=128),
                            w=[("w1b", sb)])
                        for sub in range(2):
                            b_ = pr * 2 + sub
                            for tsub in range(2):
                                hb = (b_ * 2 + tsub) % 4
                                for cc in range(NCH):
                                    mm(PS[hb][:, 0:512],
                                       W1B[:, sb * 2048 + sub * 1024 + cc * 128:
                                           sb * 2048 + sub * 1024 + (cc + 1) * 128],
                                       TNT[:, cc * 1024 + tsub * 512: cc * 1024 + (tsub + 1) * 512],
                                       cc == 0, cc == 7,
                                       r=[("w1b", sb)] + [("tnt", cc, t4)
                                                          for t4 in range(tsub * 4, tsub * 4 + 4)],
                                       w=[("ps", hb)])
                                act(HID[:, b_ * 1024 + tsub * 512: b_ * 1024 + (tsub + 1) * 512],
                                    PS[hb][:, 0:512], AF.Gelu,
                                    bias=B1T[:, e_ * 32 + b_: e_ * 32 + b_ + 1],
                                    r=[("ps", hb), ("b1t",)], w=[("hid", b_, tsub)])
                    dma("gpsimd", B2E[0:1, 0:1024], d_b2[e_:e_ + 1, :], w=[("b2e",)])
                    for g in range(2):
                        for kp in range(16):
                            sw = kp % 2
                            r0 = (((e_ * 2 + g) * 32 + kp * 2)) * 128
                            dma("gpsimd",
                                W2B[:, sw * 1024:(sw + 1) * 1024].rearrange(
                                    "p (two d) -> p two d", two=2),
                                d_w2d[r0:r0 + 256, :].rearrange(
                                    "(two p) d -> p two d", p=128),
                                w=[("w2b", sw)])
                            for sub in range(2):
                                k = kp * 2 + sub
                                for jj in range(8):
                                    mm(PS[jj][:, 0:512],
                                       HID[:, k * 1024 + jj * 128: k * 1024 + (jj + 1) * 128],
                                       W2B[:, sw * 1024 + sub * 512: sw * 1024 + (sub + 1) * 512],
                                       k == 0, False,
                                       r=[("hid", k, jj // 4), ("w2b", sw)],
                                       w=[("ps", jj)])
                        for jj in range(8):
                            mm(PS[jj][:, 0:512], ONESB[0:1, 0:128],
                               B2E[0:1, g * 512:(g + 1) * 512], False, True,
                               r=[("onesb",), ("b2e",)], w=[("ps", jj)])
                            vstt(MOE[:, jj * 1024 + g * 512: jj * 1024 + (g + 1) * 512],
                                 PS[jj][:, 0:512], GW[:, jj * 8 + e_: jj * 8 + e_ + 1],
                                 MOE[:, jj * 1024 + g * 512: jj * 1024 + (g + 1) * 512],
                                 ALU.mult, ALU.add,
                                 r=[("ps", jj), ("gw", jj), ("moe", jj, g)],
                                 w=[("moe", jj, g)])

                if DEBUG_SPILLS and s == 1 and tb == 1:
                    dma("sync", d_dbgtn, TNT[:, 0:8192],
                        r=[("tnt", cc, jj) for cc in range(NCH) for jj in range(8)],
                        out_sem=sOUT)
                    dma("sync", d_dbggw, GW[:, 0:64],
                        r=[("gw", jj) for jj in range(8)], out_sem=sOUT)
                    for jj in range(8):
                        dma("sync", d_dbgmoe[jj * 128:(jj + 1) * 128, :],
                            MOE[:, jj * 1024:(jj + 1) * 1024],
                            r=[("moe", jj, 0), ("moe", jj, 1)], out_sem=sOUT)
                    dma("sync", d_dbghid, HID[:, 0:32768],
                        r=[("hid", b_, t_) for b_ in range(32) for t_ in range(2)],
                        out_sem=sOUT)
                    n_out += 11

                def c_A(jj):
                    j = j0 + jj
                    sl = jj % 2
                    hv = HTS[:, sl * 1024:(sl + 1) * 1024]
                    dma("sync", hv, d_hsp[j * 128:(j + 1) * 128, :],
                        r=[("hsp", j)], w=[("hts", sl)])
                    fv = MOE[:, jj * 1024:(jj + 1) * 1024]
                    vtt(ALU.add, fv, hv, fv,
                        r=[("hts", sl), ("moe", jj, 0), ("moe", jj, 1)],
                        w=[("moe", jj, 0), ("moe", jj, 1)])
                    act(fv, fv, AF.Silu, r=[("moe", jj, 0)],
                        w=[("moe", jj, 0), ("moe", jj, 1)])
                    sv = SST2[:, sl * 1024:(sl + 1) * 1024]
                    dma("sync", sv, d_ssp[j * 128:(j + 1) * 128, :],
                        r=[("ssp", j)], w=[("sstr", sl)])
                    xv = XST2[:, sl * 1024:(sl + 1) * 1024]
                    dma("sync", xv, d_x[s * SEQ + j * 128: s * SEQ + (j + 1) * 128, :],
                        w=[("xstr", sl)])

                def c_B(jj):
                    j = j0 + jj
                    sl = jj % 2
                    fv = MOE[:, jj * 1024:(jj + 1) * 1024]
                    sv = SST2[:, sl * 1024:(sl + 1) * 1024]
                    xv = XST2[:, sl * 1024:(sl + 1) * 1024]
                    vtt(ALU.mult, fv, fv, sv,
                        r=[("moe", jj, 0), ("sstr", sl)],
                        w=[("moe", jj, 0), ("moe", jj, 1)])
                    ov = OUTS[:, sl * 1024:(sl + 1) * 1024]
                    vtt(ALU.add, ov, fv, xv,
                        r=[("moe", jj, 0), ("xstr", sl)], w=[("outs", sl)])
                    dma("sync", d_out[s * SEQ + j * 128: s * SEQ + (j + 1) * 128, :],
                        ov, r=[("outs", sl)], out_sem=sOUT)

                for t_ in range(9):
                    if t_ < 8:
                        c_A(t_)
                    if t_ >= 1:
                        c_B(t_ - 1)
                n_out += 8
            S.fence()
            A.release()

        stats = S.finalize_and_emit(block, final_waits=[(sOUT, n_out * 16)])
        total_ops = sum(v[0] for v in stats.values())
        print(f"[kernel] ops={stats} total={total_ops} arena_peak={A.peak}")
    return nc



def _host_tables():
    identf = np.eye(128, dtype=np.float32)
    tri = np.triu(np.full((128, 128), -1e30, np.float32), k=1)
    onesf = np.ones((1, 128), np.float32)
    onesb = np.ones((1, 128), BF)
    neg2 = np.concatenate([np.full((1, 16), -2 * LR, np.float32),
                           np.full((1, 16), -LR, np.float32)], axis=1)
    R = np.zeros((64, 64), np.float32)
    for i in range(32):
        R[i, i + 32] = -1.0
        R[i + 32, i] = 1.0
    R2 = np.zeros((128, 128), np.float32)
    R2[0:64, 0:64] = R
    R2[64:128, 64:128] = R
    r2t = R2.T.astype(BF)
    inv = (1.0 / (10000.0 ** (np.arange(0, 64, 2, dtype=np.float32) / 64.0)))
    ang = np.arange(SEQ, dtype=np.float32)[None, :] * inv[:, None]
    cos64 = np.concatenate([np.cos(ang), np.cos(ang)], axis=0)
    sin64 = np.concatenate([np.sin(ang), np.sin(ang)], axis=0)
    cosb = np.concatenate([cos64, cos64], axis=0).astype(BF)
    sinb = np.concatenate([sin64, sin64], axis=0).astype(BF)
    return identf, tri, onesf, onesb, neg2, r2t, cosb, sinb


def kernel(**inputs):
    x = np.asarray(inputs["L0_attn_x"], np.float32)
    Wb = np.ascontiguousarray(np.asarray(inputs["L0_attn_Wb"], np.float32))
    iA = np.asarray(inputs["L0_attn_init_A"], np.float32)
    iB = np.asarray(inputs["L0_attn_init_B"], np.float32)
    lng = np.asarray(inputs["L0_attn_ln_gamma"], np.float32)
    lnb = np.asarray(inputs["L0_attn_ln_beta"], np.float32)
    Wp = np.ascontiguousarray(np.asarray(inputs["L0_attn_Wp"], np.float32))
    anw = np.asarray(inputs["L0_ffn_attn_norm_w"], np.float32)
    wq = np.asarray(inputs["L0_ffn_wq"], np.float32)
    wk = np.asarray(inputs["L0_ffn_wk"], np.float32)
    wv = np.asarray(inputs["L0_ffn_wv"], np.float32)
    wo = np.asarray(inputs["L0_ffn_wo"], np.float32)
    mnw = np.asarray(inputs["L0_ffn_moe_norm_w"], np.float32)
    gw = np.asarray(inputs["L0_ffn_gate_w"], np.float32)
    w1 = np.asarray(inputs["L0_ffn_w1"], np.float32)
    b1 = np.asarray(inputs["L0_ffn_b1"], np.float32)
    w2 = np.asarray(inputs["L0_ffn_w2"], np.float32)
    b2 = np.asarray(inputs["L0_ffn_b2"], np.float32)

    identf, tri, onesf, onesb, neg2, r2t, cosb, sinb = _host_tables()

    w1d = np.ascontiguousarray(
        w1.reshape(8, 32, 128, 8, 128).transpose(0, 1, 4, 3, 2)).reshape(8 * 32 * 128, 1024)
    w2d = np.ascontiguousarray(
        w2.reshape(8, 2, 512, 32, 128).transpose(0, 1, 3, 4, 2)).reshape(8 * 2 * 32 * 128, 512)
    b1t = np.ascontiguousarray(b1.reshape(8, 32, 128).transpose(2, 0, 1)).reshape(128, 256)

    common = {
        "wbt": np.ascontiguousarray(Wb.T), "wpt": np.ascontiguousarray(Wp.T),
        "wpraw": Wp,
        "wqt": np.ascontiguousarray(wq.T), "wkt": np.ascontiguousarray(wk.T),
        "wvt": np.ascontiguousarray(wv.T), "wot": np.ascontiguousarray(wo.T),
        "w1d": w1d, "w2d": w2d, "b1t": b1t, "b2": np.ascontiguousarray(b2),
        "gatet": np.ascontiguousarray(gw.T),
        "initac": np.ascontiguousarray(iA),
        "initbtc": np.ascontiguousarray(iB.T),
        "initat": np.ascontiguousarray(iA.T),
        "initb": np.ascontiguousarray(iB),
        "lng": lng.reshape(1, D), "lnb": lnb.reshape(1, D),
        "moew": mnw.reshape(1, D),
        "attnwc": np.ascontiguousarray(anw.reshape(8, 128).T),
        "identf": identf, "tri": tri, "onesf": onesf, "onesb": onesb,
        "neg2": neg2, "r2t": r2t, "cosb": cosb, "sinb": sinb,
    }
    in_maps = []
    for core in range(8):
        m = dict(common)
        m["x"] = np.ascontiguousarray(x[2 * core:2 * core + 2].reshape(2 * SEQ, D))
        in_maps.append(m)

    nc = build_program()
    res = run_bass_kernel_spmd(nc, in_maps, core_ids=list(range(8)))
    outs = [np.asarray(r["out"], np.float32).reshape(2, SEQ, D) for r in res.results]
    return np.concatenate(outs, axis=0)


# version 08
# speedup vs baseline: 1.0663x; 1.0663x over previous
import numpy as np
import ml_dtypes
import concourse.bass as bass
import concourse.mybir as mybir
from concourse.bass_utils import run_bass_kernel_spmd

F32 = mybir.dt.float32
BF16 = mybir.dt.bfloat16
U8 = mybir.dt.uint8
AF = mybir.ActivationFunctionType
ALU = mybir.AluOpType
AX = mybir.AxisListType
BF = ml_dtypes.bfloat16

ENGINES = ("sync", "gpsimd", "tensor", "vector", "scalar")



class Op:
    __slots__ = ("eng", "fn", "dma", "need_inc", "cum", "deps", "pos", "inc_sem")

    def __init__(self, eng, fn, dma, pos):
        self.eng = eng
        self.fn = fn
        self.dma = dma
        self.need_inc = dma
        self.cum = None
        self.deps = []
        self.pos = pos
        self.inc_sem = None


class Sched:
    def __init__(self, eng_sems, dma_sems=None):
        self.eng_sems = eng_sems
        self.dma_sems = dma_sems or {}
        self.dma_cnt = {e: 0 for e in ENGINES}
        self.last_dma = {e: {} for e in ENGINES}
        self.ops = {e: [] for e in ENGINES}
        self.res = {}
        self.fence_ops = []

    def op(self, eng, fn, r=(), w=(), extra=(), dma=False, out_sem=None):
        o = Op(eng, fn, dma, len(self.ops[eng]))
        if dma and out_sem is None and eng in self.dma_sems:
            pool = self.dma_sems[eng]
            i = self.dma_cnt[eng] % len(pool)
            self.dma_cnt[eng] += 1
            o.inc_sem = pool[i]
            self.last_dma[eng][i] = o
        deps = list(extra) + list(self.fence_ops)
        for k in r:
            e = self.res.get(k)
            if e is not None and e[0] is not None:
                deps.append(e[0])
        for k in w:
            e = self.res.get(k)
            if e is not None:
                deps.extend(e[1])
                if e[0] is not None:
                    deps.append(e[0])
        best = {}
        for d in deps:
            if d is None or d is o:
                continue
            if d.eng == "tensor" and eng == "tensor":
                continue
            key = (d.eng, d.dma, id(d.inc_sem))
            b = best.get(key)
            if b is None or d.pos > b.pos:
                best[key] = d
        for d in best.values():
            d.need_inc = True
            o.deps.append(d)
        for k in r:
            self.res.setdefault(k, [None, []])[1].append(o)
        for k in w:
            self.res[k] = [o, []]
        if out_sem is not None:
            o.inc_sem = out_sem
            o.need_inc = True
        self.ops[eng].append(o)
        return o

    def fence(self):
        fo = []
        for e, ops in self.ops.items():
            if not ops:
                continue
            fo.append(ops[-1])
            fo.extend(self.last_dma[e].values())
        self.fence_ops = fo

    def finalize_and_emit(self, block, final_waits=()):
        cum = {}
        for e in ENGINES:
            for o in self.ops[e]:
                sem = o.inc_sem
                if sem is None:
                    sem = self.eng_sems[e]
                o.inc_sem = sem
                if o.need_inc:
                    cum[id(sem)] = cum.get(id(sem), 0) + (16 if o.dma else 1)
                o.cum = cum.get(id(sem), 0)
        stats = {e: (len(self.ops[e]), sum(1 for o in self.ops[e] if o.need_inc))
                 for e in ENGINES}
        for ename in ENGINES:
            def mk(name, olist):
                def _(eng):
                    lw = {}
                    for o in olist:
                        for d in o.deps:
                            if d.cum <= lw.get(id(d.inc_sem), 0):
                                continue
                            eng.wait_ge(d.inc_sem, d.cum)
                            lw[id(d.inc_sem)] = d.cum
                        inst = o.fn(eng)
                        if o.need_inc:
                            inst.then_inc(o.inc_sem, 16 if o.dma else 1)
                    if name == "sync":
                        for (sem, val) in final_waits:
                            eng.wait_ge(sem, val)
                return _
            getattr(block, ename)(mk(ename, self.ops[ename]))
        return stats


class View:
    __slots__ = ("ap", "off", "cols")

    def __init__(self, ap, off, cols):
        self.ap = ap
        self.off = off
        self.cols = cols

    def __getitem__(self, idx):
        p, c = idx
        if isinstance(c, slice):
            c0 = 0 if c.start is None else c.start
            c1 = self.cols if c.stop is None else c.stop
        else:
            c0, c1 = c, c + 1
        assert 0 <= c0 < c1 <= self.cols, f"view col range {c0}:{c1} of {self.cols}"
        return self.ap[p, c0:c1]


class Arena:

    def __init__(self, nc, handle, total):
        self.nc = nc
        self.base = nc.lookup_mloc(handle).addr
        self.total = total
        self.off = 0
        self.stack = []
        self.n = 0
        self.peak = 0

    def t(self, dtype, cols, tag=""):
        nbytes = cols * (4 if dtype == F32 else 2)
        nbytes = (nbytes + 31) & ~31
        assert self.off + nbytes <= self.total, \
            f"SBUF arena overflow at {tag}: {self.off + nbytes} > {self.total}"
        name = f"ar{self.n}_{tag}"
        self.n += 1
        hnd = self.nc.virtual_sbuf_tensor(name, [128, cols], dtype)
        self.nc._tensor(hnd.name, list(hnd.shape), hnd.dtype, type="SB")
        mloc = self.nc.lookup_mloc(hnd)
        mloc.allocated = True
        mloc.addr = self.base + self.off
        self.off += nbytes
        self.peak = max(self.peak, self.off)
        return View(hnd.ap(), 0, cols)

    def mark(self):
        self.stack.append(self.off)

    def release(self):
        self.off = self.stack.pop()



SEQ = 2048
D = 1024
NT = 16
NCH = 8
CH = 8
NH = 16
HD = 64
NE = 8
H4 = 4096
LR = 0.02
DEBUG_SPILLS = False


def build_program():
    nc = bass.Bass("TRN2", target_bir_lowering=False)

    def din(name, shape, dt=F32):
        return nc.dram_tensor(name, shape, dt, kind="ExternalInput").ap()

    d_x = din("x", [2 * SEQ, D])
    d_wbt = din("wbt", [D, D])
    d_wpt = din("wpt", [D, D])
    d_wpraw = din("wpraw", [D, D])
    d_wqt = din("wqt", [D, D])
    d_wkt = din("wkt", [D, D])
    d_wvt = din("wvt", [D, D])
    d_wot = din("wot", [D, D])
    d_w1d = din("w1d", [NE * 32 * 128, 1024])
    d_w2d = din("w2d", [NE * 2 * 32 * 128, 512])
    d_b1t = din("b1t", [128, 256])
    d_b2 = din("b2", [NE, D])
    d_gatet = din("gatet", [D, 8])
    d_initac = din("initac", [D, 16])
    d_initbtc = din("initbtc", [D, 16])
    d_initat = din("initat", [16, D])
    d_initb = din("initb", [16, D])
    d_lng = din("lng", [1, D])
    d_lnb = din("lnb", [1, D])
    d_moew = din("moew", [1, D])
    d_attnwc = din("attnwc", [128, 8])
    d_identf = din("identf", [128, 128])
    d_tri = din("tri", [128, 128])
    d_onesf = din("onesf", [1, 128])
    d_onesb = din("onesb", [1, 128], BF16)
    d_neg2 = din("neg2", [1, 32])
    d_r2t = din("r2t", [128, 128], BF16)
    d_cosb = din("cosb", [128, SEQ], BF16)
    d_sinb = din("sinb", [128, SEQ], BF16)

    d_out = nc.dram_tensor("out", [2 * SEQ, D], F32, kind="ExternalOutput").ap()
    kd = "ExternalOutput" if DEBUG_SPILLS else "Internal"
    d_r1sp = nc.dram_tensor("r1sp", [1, SEQ], F32, kind=kd).ap()
    d_xnsp = nc.dram_tensor("xnsp", [SEQ, D], F32, kind=kd).ap()
    d_xtmsp = nc.dram_tensor("xtmsp", [SEQ, D], BF16, kind=kd).ap()
    d_ssp = nc.dram_tensor("ssp", [SEQ, D], F32, kind=kd).ap()
    d_hsp = nc.dram_tensor("hsp", [SEQ, D], F32, kind=kd).ap()
    if DEBUG_SPILLS:
        d_dbgtt = nc.dram_tensor("dbgtt", [128, 16384], BF16, kind=kd).ap()
        d_dbgq = nc.dram_tensor("dbgq", [128, 16384], BF16, kind=kd).ap()
        d_dbgk = nc.dram_tensor("dbgk", [128, 16384], BF16, kind=kd).ap()
        d_dbgv = nc.dram_tensor("dbgv", [128, 16384], BF16, kind=kd).ap()
        d_dbgao = nc.dram_tensor("dbgao", [128, 16384], BF16, kind=kd).ap()
        d_dbgtn = nc.dram_tensor("dbgtn", [128, 8192], BF16, kind=kd).ap()
        d_dbggw = nc.dram_tensor("dbggw", [128, 64], F32, kind=kd).ap()
        d_dbgmoe = nc.dram_tensor("dbgmoe", [1024, D], F32, kind=kd).ap()
        d_dbghid = nc.dram_tensor("dbghid", [128, 32768], BF16, kind=kd).ap()

    ARENA_BYTES = 181248

    import contextlib
    with contextlib.ExitStack() as es:
        ARH = es.enter_context(nc.sbuf_tensor("arena", [128, ARENA_BYTES], U8))
        PS = [es.enter_context(nc.psum_tensor(f"ps{i}", [128, 512], F32))
              for i in range(8)]
        sSY = es.enter_context(nc.semaphore("sSY"))
        sGP = es.enter_context(nc.semaphore("sGP"))
        sPE = es.enter_context(nc.semaphore("sPE"))
        sDV = es.enter_context(nc.semaphore("sDV"))
        sAC = es.enter_context(nc.semaphore("sAC"))
        sOUT = es.enter_context(nc.semaphore("sOUT"))
        block = es.enter_context(nc.Block())
        A = Arena(nc, ARH, ARENA_BYTES)
        NPOOL = 8
        pool_sy = [es.enter_context(nc.semaphore(f"dsy{i}")) for i in range(NPOOL)]
        pool_gp = [es.enter_context(nc.semaphore(f"dgp{i}")) for i in range(NPOOL)]
        pool_st = [es.enter_context(nc.semaphore(f"dst{i}")) for i in range(NPOOL)]
        S = Sched({"sync": sSY, "gpsimd": sGP, "tensor": sPE,
                   "vector": sDV, "scalar": sAC},
                  dma_sems={"sync": pool_sy, "gpsimd": pool_gp,
                            "scalar": pool_st})

        def dma(eng, dst, src, r=(), w=(), out_sem=None):
            return S.op(eng, lambda e, d=dst, s=src: e.dma_start(out=d, in_=s),
                        r=r, w=w, dma=True, out_sem=out_sem)

        def mm(out, lhsT, rhs, start, stop, r=(), w=(), tp=None):
            return S.op("tensor",
                        lambda e, o=out, l=lhsT, rr=rhs, st=start, sp=stop, t=tp:
                        e.matmul(o, l, rr, start=st, stop=sp, tile_position=t),
                        r=r, w=w)

        def trans(out, in_, ident, r=(), w=()):
            return S.op("tensor",
                        lambda e, o=out, i=in_, d=ident: e.transpose(o, i, d),
                        r=r, w=w)

        def vcp(out, in_, r=(), w=()):
            return S.op("vector", lambda e, o=out, i=in_: e.tensor_copy(o, i),
                        r=r, w=w)

        def vtt(op, out, in0, in1, r=(), w=()):
            return S.op("vector",
                        lambda e, o=out, a=in0, b=in1, p=op: e.tensor_tensor(o, a, b, p),
                        r=r, w=w)

        def vts(out, in0, s1, s2, op0, op1=None, r=(), w=()):
            if op1 is None:
                return S.op("vector",
                            lambda e, o=out, a=in0, x=s1, p=op0:
                            e.tensor_scalar(o, a, x, None, p),
                            r=r, w=w)
            return S.op("vector",
                        lambda e, o=out, a=in0, x=s1, y=s2, p=op0, q=op1:
                        e.tensor_scalar(o, a, x, y, p, q),
                        r=r, w=w)

        def vstt(out, in0, scal, in1, op0, op1, r=(), w=()):
            return S.op("vector",
                        lambda e, o=out, a=in0, x=scal, b=in1, p=op0, q=op1:
                        e.scalar_tensor_tensor(o, a, x, b, p, q),
                        r=r, w=w)

        def act(out, in_, func, bias=None, scale=None, accum=None, r=(), w=()):
            def f(e, o=out, i=in_, fn=func, b=bias, sc=scale, a=accum):
                kw = {}
                if b is not None:
                    kw["bias"] = b
                if sc is not None:
                    kw["scale"] = sc
                if a is not None:
                    kw["accum_out"] = a
                return e.activation(o, i, fn, **kw)
            return S.op("scalar", f, r=r, w=w)

        def rmax(out, in_, r=(), w=()):
            return S.op("vector", lambda e, o=out, i=in_: e.reduce_max(o, i, axis=AX.X),
                        r=r, w=w)

        def rsum(out, in_, r=(), w=()):
            return S.op("vector", lambda e, o=out, i=in_: e.reduce_sum(o, i, axis=AX.X),
                        r=r, w=w)

        def vrec(out, in_, r=(), w=()):
            return S.op("vector", lambda e, o=out, i=in_: e.reciprocal(o, i), r=r, w=w)

        def vmemset(ap_, val, r=(), w=()):
            return S.op("vector", lambda e, a=ap_, v=val: e.memset(a, v), r=r, w=w)

        def vbn(out, in_, r=(), w=()):
            return S.op("vector", lambda e, o=out, i=in_: e.bn_stats(o, i), r=r, w=w)

        def vbnagg(out, in_, r=(), w=()):
            return S.op("vector", lambda e, o=out, i=in_: e.bn_aggr(o, i), r=r, w=w)

        def vttr(out, in0, in1, accum, r=(), w=()):
            return S.op("vector",
                        lambda e, o=out, a=in0, b=in1, ac=accum:
                        e.tensor_tensor_reduce(o, a, b, 1.0, 0.0, ALU.mult,
                                               ALU.add, ac),
                        r=r, w=w)

        IDF = A.t(F32, 128, "idf")
        TRI = A.t(F32, 128, "tri")
        ONESF = A.t(F32, 128, "onesf")
        ONESB = A.t(BF16, 128, "onesb")
        NEG2 = A.t(F32, 32, "neg2")
        ATTNW = A.t(F32, 16, "attnw")
        B1T = A.t(F32, 256, "b1t")
        R2T = A.t(BF16, 128, "r2t")
        GATET = A.t(BF16, 64, "gatet")
        WPTA = A.t(BF16, 128, "wpta")
        INITBT = A.t(BF16, 128, "initbt")

        dma("sync", IDF[:, :], d_identf, w=[("idf",)])
        dma("sync", TRI[:, :], d_tri, w=[("tri",)])
        dma("sync", ONESF[0:1, :], d_onesf, w=[("onesf",)])
        dma("sync", ONESB[0:1, :], d_onesb, w=[("onesb",)])
        dma("sync", NEG2[0:1, :], d_neg2, w=[("neg2",)])
        dma("sync", ATTNW[:, 0:8], d_attnwc, w=[("attnw",)])
        dma("sync", B1T[:, :], d_b1t, w=[("b1t",)])
        dma("sync", R2T[:, :], d_r2t, w=[("r2t",)])
        for c in range(NCH):
            dma("gpsimd", GATET[:, c * 8:(c + 1) * 8],
                d_gatet[c * 128:(c + 1) * 128, :], w=[("gatet",)])
            dma("gpsimd", INITBT[:, c * 16:(c + 1) * 16],
                d_initbtc[c * 128:(c + 1) * 128, :], w=[("initbt",)])

        A.mark()
        WPR = A.t(BF16, 8192, "wpr")
        INA = A.t(BF16, 128, "ina")
        for c in range(NCH):
            dma("gpsimd", WPR[:, c * 1024:(c + 1) * 1024],
                d_wpraw[c * 128:(c + 1) * 128, :], w=[("wpr", c)])
            dma("gpsimd", INA[:, c * 16:(c + 1) * 16],
                d_initac[c * 128:(c + 1) * 128, :], w=[("ina",)])
        for m in range(8):
            for hc in range(NCH):
                mm(PS[0][:, 0:16], WPR[:, hc * 1024 + m * 128: hc * 1024 + (m + 1) * 128],
                   INA[:, hc * 16:(hc + 1) * 16], hc == 0, hc == 7,
                   r=[("wpr", hc), ("ina",)], w=[("ps", 0)])
            vcp(WPTA[:, m * 16:(m + 1) * 16], PS[0][:, 0:16],
                r=[("ps", 0)], w=[("wpta",)])
        S.fence()
        A.release()

        n_out = 0

        for s in range(2):
            A.mark()
            XNT = A.t(BF16, 16384, "xnt")

            A.mark()
            SHT = A.t(BF16, 16384, "sht")
            GAMB = A.t(F32, D, "gamb")
            BETB = A.t(F32, D, "betb")
            GROW = A.t(F32, D, "grow")
            XST = A.t(F32, 2048, "xst")
            XNS = A.t(F32, 2048, "xns")
            X2A = A.t(F32, D, "x2a")
            X2B = A.t(F32, D, "x2b")
            XTMC = A.t(BF16, 2048, "xtmc")
            BN = A.t(F32, 16, "bn")
            MV = A.t(F32, 8, "mv")
            SMALL = A.t(F32, 32, "small")
            RSTD1C = A.t(F32, 16, "r1c")
            R1ROW = A.t(F32, SEQ, "r1row")

            for (drow, dst, tag) in ((d_lng, GAMB, "gamb"), (d_lnb, BETB, "betb")):
                dma("sync", GROW[0:1, :], drow, r=[("grow",)], w=[("grow",)])
                for nb in range(2):
                    mm(PS[0][:, 0:512], ONESF[0:1, :],
                       GROW[0:1, nb * 512:(nb + 1) * 512], True, True,
                       r=[("onesf",), ("grow",)], w=[("ps", 0)])
                    vcp(dst[:, nb * 512:(nb + 1) * 512], PS[0][:, 0:512],
                        r=[("ps", 0)], w=[(tag,)])

            for c in range(NCH):
                vmemset(SHT[:, c * 2048 + 2047: c * 2048 + 2048], 0.0,
                        w=[("sht", c, 16)])

            for j in range(NT):
                sl = j % 2
                xv = XST[:, sl * 1024:(sl + 1) * 1024]
                dma("sync", xv, d_x[s * SEQ + j * 128: s * SEQ + (j + 1) * 128, :],
                    w=[("xst", sl)])
                vbn(BN[:, 0:6], XST[:, sl * 1024: sl * 1024 + 512], r=[("xst", sl)], w=[("bn",)])
                vbn(BN[:, 6:12], XST[:, sl * 1024 + 512: (sl + 1) * 1024], r=[("xst", sl)], w=[("bn",)])
                vbnagg(MV[:, 0:2], BN[:, 0:12], r=[("bn",)], w=[("mv",)])
                vts(SMALL[:, 8:9], MV[:, 1:2], 1e-5, None, ALU.add, r=[("mv",)], w=[("sq1a",)])
                act(SMALL[:, 0:1], SMALL[:, 8:9], AF.Sqrt, r=[("sq1a",)], w=[("sq1",)])
                vrec(SMALL[:, 1:2], SMALL[:, 0:1], r=[("sq1",)], w=[("rstd",)])
                xnv = XNS[:, sl * 1024:(sl + 1) * 1024]
                vts(xnv, xv, MV[:, 0:1], SMALL[:, 1:2], ALU.subtract, ALU.mult,
                    r=[("xst", sl), ("mv",), ("rstd",)], w=[("xns", sl)])
                dma("scalar", d_xnsp[j * 128:(j + 1) * 128, :], xnv,
                    r=[("xns", sl)], w=[("xnsp", j)])
                xtv = XTMC[:, sl * 1024:(sl + 1) * 1024]
                vcp(xtv, xnv, r=[("xns", sl)], w=[("xtmc", sl)])
                dma("scalar", d_xtmsp[j * 128:(j + 1) * 128, :], xtv,
                    r=[("xtmc", sl)], w=[("xtmsp", j)])
                for c in range(NCH):
                    b = 2 + (c % 2)
                    trans(PS[b][:, 0:128], XNS[:, sl * 1024 + c * 128: sl * 1024 + (c + 1) * 128],
                          IDF[:, :], r=[("xns", sl), ("idf",)], w=[("ps", b)])
                    vcp(XNT[:, c * 2048 + j * 128: c * 2048 + (j + 1) * 128],
                        PS[b][:, 0:128], r=[("ps", b)], w=[("xnt", c, j)])
                vbn(BN[:, 0:6], XNS[:, sl * 1024: sl * 1024 + 512], r=[("xns", sl)], w=[("bn",)])
                vbn(BN[:, 6:12], XNS[:, sl * 1024 + 512: (sl + 1) * 1024], r=[("xns", sl)], w=[("bn",)])
                vbnagg(MV[:, 2:4], BN[:, 0:12], r=[("bn",)], w=[("mv2",)])
                vts(SMALL[:, 9:10], MV[:, 3:4], 1e-5, None, ALU.add, r=[("mv2",)], w=[("sq2a",)])
                act(SMALL[:, 2:3], SMALL[:, 9:10], AF.Sqrt, r=[("sq2a",)], w=[("sq2",)])
                vrec(SMALL[:, 3:4], SMALL[:, 2:3], r=[("sq2",)], w=[("rstd2",)])
                vtt(ALU.mult, SMALL[:, 4:5], MV[:, 2:3], MV[:, 2:3], r=[("mv2",)], w=[("m2sq",)])
                vtt(ALU.add, SMALL[:, 5:6], SMALL[:, 4:5], MV[:, 3:4],
                    r=[("m2sq",), ("mv2",)], w=[("msq",)])
                vts(SMALL[:, 10:11], SMALL[:, 5:6], 1e-6, None, ALU.add, r=[("msq",)], w=[("sq3a",)])
                act(SMALL[:, 6:7], SMALL[:, 10:11], AF.Sqrt, r=[("sq3a",)], w=[("sq3",)])
                vrec(RSTD1C[:, j: j + 1], SMALL[:, 6:7], r=[("sq3",)], w=[("r1c", j)])
                vts(X2A[:, 0:1024], xnv, MV[:, 2:3], SMALL[:, 3:4], ALU.subtract, ALU.mult,
                    r=[("xns", sl), ("mv2",), ("rstd2",)], w=[("x2a",)])
                vtt(ALU.mult, X2B[:, 0:1024], X2A[:, 0:1024], GAMB[:, 0:1024],
                    r=[("x2a",), ("gamb",)], w=[("x2b",)])
                vtt(ALU.add, X2B[:, 0:1024], X2B[:, 0:1024], BETB[:, 0:1024],
                    r=[("x2b",), ("betb",)], w=[("x2b",)])
                for c in range(NCH):
                    b = 2 + (c % 2)
                    trans(PS[b][:, 0:128], X2B[:, c * 128:(c + 1) * 128],
                          IDF[:, :], r=[("x2b",), ("idf",)], w=[("ps", b)])
                    if j == 0:
                        vcp(SHT[:, c * 2048 + 0: c * 2048 + 127],
                            PS[b][:, 1:128], r=[("ps", b)], w=[("sht", c, 0)])
                    else:
                        vcp(SHT[:, c * 2048 + j * 128 - 1: c * 2048 + j * 128 + 127],
                            PS[b][:, 0:128], r=[("ps", b)], w=[("sht", c, j)])

            for j in range(NT):
                trans(PS[4][0:1, 0:128], RSTD1C[:, j: j + 1], IDF[:, :],
                      r=[("r1c", j), ("idf",)], w=[("ps", 4)])
                vcp(R1ROW[0:1, j * 128:(j + 1) * 128], PS[4][0:1, 0:128],
                    r=[("ps", 4)], w=[("r1row",)])
            dma("scalar", d_r1sp, R1ROW[0:1, 0:SEQ], r=[("r1row",)], w=[("r1sp",)])

            S.fence()
            A.release()

            A.mark()
            SHTb = A.t(BF16, 16384, "sht2")
            WBT = A.t(BF16, 8192, "wbtv")
            WPT = A.t(BF16, 8192, "wptv")
            VCH = A.t(BF16, 4096, "vch")
            XTM = A.t(BF16, 4096, "xtm")
            SST = A.t(F32, 2048, "sst")
            PIN = A.t(BF16, 32, "pin")
            PERR = A.t(BF16, 32, "perr")
            MIDT = A.t(BF16, 256, "midt")
            AEF = A.t(BF16, D, "aeff")
            BEFFT = A.t(BF16, 128, "befft")
            IAT = A.t(F32, D, "iat")
            IAT2 = A.t(F32, D, "iat2")
            IB = A.t(F32, D, "ib")
            ACCA = A.t(F32, D, "acca")
            ACCB = A.t(F32, D, "accb")
            TSCR = A.t(F32, D, "tscr")
            BEFF = A.t(F32, D, "beff")
            SM = A.t(F32, 48, "sm")

            for c in range(NCH):
                dma("gpsimd", WBT[:, c * 1024:(c + 1) * 1024],
                    d_wbt[c * 128:(c + 1) * 128, :], w=[("wbt", c)])
                dma("gpsimd", WPT[:, c * 1024:(c + 1) * 1024],
                    d_wpt[c * 128:(c + 1) * 128, :], w=[("wpt", c)])
            dma("sync", IAT[0:16, :], d_initat, w=[("iat",)])
            dma("sync", IB[0:16, :], d_initb, w=[("ib",)])
            vts(IAT2[0:16, :], IAT[0:16, :], 2.0, None, ALU.mult,
                r=[("iat",)], w=[("iat2",)])
            vmemset(ACCA[0:16, :], 0.0, w=[("acca",)])
            vmemset(ACCB[0:16, :], 0.0, w=[("accb",)])

            for c in range(CH):
                vs = c % 2
                for tt in range(2):
                    jt = 2 * c + tt
                    dma("sync", XTM[:, (vs * 2 + tt) * 1024:(vs * 2 + tt + 1) * 1024],
                        d_xtmsp[jt * 128:(jt + 1) * 128, :],
                        r=[("xtmsp", jt)], w=[("xtm", vs, tt)])
                for tt in range(2):
                    jt = 2 * c + tt
                    for blk in range(2):
                        b = (tt * 2 + blk) % 2
                        for cc in range(NCH):
                            mm(PS[b][:, 0:512],
                               SHTb[:, cc * 2048 + jt * 128: cc * 2048 + (jt + 1) * 128],
                               WPT[:, cc * 1024 + blk * 512: cc * 1024 + (blk + 1) * 512],
                               cc == 0, cc == 7,
                               r=[("sht", cc, jt), ("sht", cc, jt + 1), ("wpt", cc)],
                               w=[("ps", b)])
                        vcp(VCH[:, (vs * 2 + tt) * 1024 + blk * 512:
                                (vs * 2 + tt) * 1024 + (blk + 1) * 512],
                            PS[b][:, 0:512], r=[("ps", b)], w=[("vch", vs, tt, blk)])

                vtt(ALU.mult, TSCR[0:16, :], ACCA[0:16, :], ACCA[0:16, :],
                    r=[("acca",)], w=[("tscr",)])
                rsum(SM[0:16, 0:1], TSCR[0:16, :], r=[("tscr",)], w=[("sa",)])
                vtt(ALU.mult, TSCR[0:16, :], ACCB[0:16, :], ACCB[0:16, :],
                    r=[("accb",), ("sa",)], w=[("tscr",)])
                rsum(SM[0:16, 1:2], TSCR[0:16, :], r=[("tscr",)], w=[("sb",)])
                vtt(ALU.add, SM[0:16, 2:3], SM[0:16, 0:1], SM[0:16, 1:2],
                    r=[("sa",), ("sb",)], w=[("sab",)])
                trans(PS[7][0:1, 0:16], SM[0:16, 2:3], IDF[0:16, 0:16],
                      r=[("sab",), ("idf",)], w=[("ps", 7)])
                vcp(SM[0:1, 16:32], PS[7][0:1, 0:16], r=[("ps", 7)], w=[("row16",)])
                rsum(SM[0:1, 8:9], SM[0:1, 16:32], r=[("row16",)], w=[("ssq",)])
                act(SM[0:1, 9:10], SM[0:1, 8:9], AF.Sqrt, scale=LR * LR,
                    r=[("ssq",)], w=[("tn",)])
                vts(SM[0:1, 10:11], SM[0:1, 9:10], 1e-6, None, ALU.add,
                    r=[("tn",)], w=[("tne",)])
                vrec(SM[0:1, 12:13], SM[0:1, 10:11], r=[("tne",)], w=[("rec",)])
                vts(SM[0:1, 11:12], SM[0:1, 12:13], 1.0, None, ALU.min,
                    r=[("rec",)], w=[("coef",)])
                mm(PS[7][0:16, 0:1], NEG2[0:1, 0:16], SM[0:1, 11:12], True, True,
                   r=[("neg2",), ("coef",)], w=[("ps", 7)])
                vcp(SM[0:16, 3:4], PS[7][0:16, 0:1], r=[("ps", 7)], w=[("c2a",)])
                mm(PS[7][0:16, 0:1], NEG2[0:1, 16:32], SM[0:1, 11:12], True, True,
                   r=[("neg2",), ("coef",)], w=[("ps", 7)])
                vcp(SM[0:16, 4:5], PS[7][0:16, 0:1], r=[("ps", 7)], w=[("c2b",)])
                vstt(AEF[0:16, :], ACCA[0:16, :], SM[0:16, 3:4], IAT2[0:16, :],
                     ALU.mult, ALU.add, r=[("acca",), ("c2a",), ("iat2",)], w=[("aef",)])
                vstt(BEFF[0:16, :], ACCB[0:16, :], SM[0:16, 4:5], IB[0:16, :],
                     ALU.mult, ALU.add, r=[("accb",), ("c2b",), ("ib",)], w=[("beff",)])
                for cc in range(NCH):
                    trans(PS[7][:, 0:16], BEFF[0:16, cc * 128:(cc + 1) * 128],
                          IDF[0:16, 0:16], r=[("beff",), ("idf",)], w=[("ps", 7)])
                    vcp(BEFFT[:, cc * 16:(cc + 1) * 16], PS[7][:, 0:16],
                        r=[("ps", 7)], w=[("befft", cc)])

                for tt in range(2):
                    jt = 2 * c + tt
                    for cc in range(NCH):
                        mm(PS[4][0:16, 0:128], BEFFT[:, cc * 16:(cc + 1) * 16],
                           XNT[:, cc * 2048 + jt * 128: cc * 2048 + (jt + 1) * 128],
                           cc == 0, cc == 7,
                           r=[("befft", cc), ("xnt", cc, jt)], w=[("ps", 4)])
                    vcp(MIDT[0:16, tt * 128:(tt + 1) * 128], PS[4][0:16, 0:128],
                        r=[("ps", 4)], w=[("midt", tt)])
                for tt in range(2):
                    jt = 2 * c + tt
                    ss = jt % 2
                    for blk in range(2):
                        b = 5 + blk
                        for cc in range(NCH):
                            mm(PS[b][:, 0:512],
                               XNT[:, cc * 2048 + jt * 128: cc * 2048 + (jt + 1) * 128],
                               WBT[:, cc * 1024 + blk * 512: cc * 1024 + (blk + 1) * 512],
                               cc == 0, False,
                               r=[("xnt", cc, jt), ("wbt", cc)], w=[("ps", b)])
                        mm(PS[b][:, 0:512], MIDT[0:16, tt * 128:(tt + 1) * 128],
                           AEF[0:16, blk * 512:(blk + 1) * 512], False, True,
                           r=[("midt", tt), ("aef",)], w=[("ps", b)])
                        vcp(SST[:, ss * 1024 + blk * 512: ss * 1024 + (blk + 1) * 512],
                            PS[b][:, 0:512], r=[("ps", b)], w=[("sst", ss, blk)])
                    dma("scalar", d_ssp[jt * 128:(jt + 1) * 128, :],
                        SST[:, ss * 1024:(ss + 1) * 1024],
                        r=[("sst", ss, 0), ("sst", ss, 1)], w=[("ssp", jt)])

                for tt in range(2):
                    jt = 2 * c + tt
                    b = tt % 2
                    for cc in range(NCH):
                        mm(PS[b][:, 0:16],
                           XNT[:, cc * 2048 + jt * 128: cc * 2048 + (jt + 1) * 128],
                           INITBT[:, cc * 16:(cc + 1) * 16], cc == 0, cc == 7,
                           r=[("xnt", cc, jt), ("initbt",)], w=[("ps", b)])
                    vcp(PIN[:, tt * 16:(tt + 1) * 16], PS[b][:, 0:16],
                        r=[("ps", b)], w=[("pin", tt)])
                    for cc in range(NCH):
                        mm(PS[2 + b][:, 0:16],
                           SHTb[:, cc * 2048 + jt * 128: cc * 2048 + (jt + 1) * 128],
                           WPTA[:, cc * 16:(cc + 1) * 16], cc == 0, cc == 7,
                           r=[("sht", cc, jt), ("sht", cc, jt + 1), ("wpta",)],
                           w=[("ps", 2 + b)])
                    vcp(PERR[:, tt * 16:(tt + 1) * 16], PS[2 + b][:, 0:16],
                        r=[("ps", 2 + b)], w=[("perr", tt)])

                for nb in range(2):
                    b = 2 + nb
                    for tt in range(2):
                        mm(PS[b][0:16, 0:512], PIN[:, tt * 16:(tt + 1) * 16],
                           VCH[:, (vs * 2 + tt) * 1024 + nb * 512:
                               (vs * 2 + tt) * 1024 + (nb + 1) * 512],
                           tt == 0, tt == 1,
                           r=[("pin", tt), ("vch", vs, tt, nb)], w=[("ps", b)])
                    vtt(ALU.add, ACCA[0:16, nb * 512:(nb + 1) * 512],
                        ACCA[0:16, nb * 512:(nb + 1) * 512], PS[b][0:16, 0:512],
                        r=[("acca",), ("ps", b)], w=[("acca",)])
                for nb in range(2):
                    b = 2 + nb
                    for tt in range(2):
                        mm(PS[b][0:16, 0:512], PERR[:, tt * 16:(tt + 1) * 16],
                           XTM[:, (vs * 2 + tt) * 1024 + nb * 512:
                               (vs * 2 + tt) * 1024 + (nb + 1) * 512],
                           tt == 0, tt == 1,
                           r=[("perr", tt), ("xtm", vs, tt)], w=[("ps", b)])
                    vtt(ALU.add, ACCB[0:16, nb * 512:(nb + 1) * 512],
                        ACCB[0:16, nb * 512:(nb + 1) * 512], PS[b][0:16, 0:512],
                        r=[("accb",), ("ps", b)], w=[("accb",)])

            S.fence()
            A.release()

            A.mark()
            A.mark()
            R1B = A.t(F32, SEQ, "r1b")
            R1R2 = A.t(F32, SEQ, "r1r2")
            dma("sync", R1R2[0:1, 0:SEQ], d_r1sp, r=[("r1sp",)], w=[("r1r2",)])
            for nb in range(4):
                mm(PS[5][:, 0:512], ONESF[0:1, :],
                   R1R2[0:1, nb * 512:(nb + 1) * 512], True, True,
                   r=[("onesf",), ("r1r2",)], w=[("ps", 5)])
                vcp(R1B[:, nb * 512:(nb + 1) * 512], PS[5][:, 0:512],
                    r=[("ps", 5)], w=[("r1b",)])
            for c in range(NCH):
                vstt(XNT[:, c * 2048:(c + 1) * 2048], XNT[:, c * 2048:(c + 1) * 2048],
                     ATTNW[:, c: c + 1], R1B[:, 0:SEQ], ALU.mult, ALU.mult,
                     r=[("xnt", c, j) for j in range(NT)] + [("attnw",), ("r1b",)],
                     w=[("tt", c)])
            S.fence()
            A.release()

            QT = A.t(BF16, 16384, "qt")
            KT = A.t(BF16, 16384, "kt")
            VTOK = A.t(BF16, 16384, "vtok")
            A.mark()
            WSTR = A.t(BF16, 8192, "wstr")
            QRAW = A.t(BF16, 1024, "qraw")
            RT1 = A.t(F32, 1024, "rt1")
            RT2 = A.t(F32, 1024, "rt2")
            COSB = A.t(BF16, SEQ, "cosb")
            SINB = A.t(BF16, SEQ, "sinb")
            dma("sync", COSB[:, :], d_cosb, w=[("cosb",)])
            dma("sync", SINB[:, :], d_sinb, w=[("sinb",)])

            for (dw, dstv, tag) in ((d_wqt, QT, "qt"), (d_wkt, KT, "kt")):
                for cc in range(NCH):
                    dma("gpsimd", WSTR[:, cc * 1024:(cc + 1) * 1024],
                        dw[cc * 128:(cc + 1) * 128, :], w=[("wstr", cc)])
                for m in range(NCH):
                    for tb in range(4):
                        bq = m % 2
                        for cc in range(NCH):
                            mm(PS[bq][:, 0:512],
                               WSTR[:, cc * 1024 + m * 128: cc * 1024 + (m + 1) * 128],
                               XNT[:, cc * 2048 + tb * 512: cc * 2048 + (tb + 1) * 512],
                               cc == 0, cc == 7,
                               r=[("wstr", cc), ("tt", cc)], w=[("ps", bq)])
                        qs = (m * 4 + tb) % 2
                        vcp(QRAW[:, qs * 512:(qs + 1) * 512], PS[bq][:, 0:512],
                            r=[("ps", bq)], w=[("qraw", qs)])
                        br = 2 + qs
                        mm(PS[br][:, 0:512], R2T[:, 0:128], QRAW[:, qs * 512:(qs + 1) * 512],
                           True, True, r=[("r2t",), ("qraw", qs)], w=[("ps", br)])
                        vtt(ALU.mult, RT1[:, qs * 512:(qs + 1) * 512],
                            QRAW[:, qs * 512:(qs + 1) * 512],
                            COSB[:, tb * 512:(tb + 1) * 512],
                            r=[("qraw", qs), ("cosb",)], w=[("rt1", qs)])
                        vtt(ALU.mult, RT2[:, qs * 512:(qs + 1) * 512],
                            PS[br][:, 0:512], SINB[:, tb * 512:(tb + 1) * 512],
                            r=[("ps", br), ("sinb",)], w=[("rt2", qs)])
                        vtt(ALU.add, dstv[:, m * 2048 + tb * 512: m * 2048 + (tb + 1) * 512],
                            RT1[:, qs * 512:(qs + 1) * 512], RT2[:, qs * 512:(qs + 1) * 512],
                            r=[("rt1", qs), ("rt2", qs)], w=[(tag, m, tb)])
            for cc in range(NCH):
                dma("gpsimd", WSTR[:, cc * 1024:(cc + 1) * 1024],
                    d_wvt[cc * 128:(cc + 1) * 128, :], w=[("wstr", cc)])
            for j in range(NT):
                for blk in range(2):
                    b = (j * 2 + blk) % 2
                    for cc in range(NCH):
                        mm(PS[b][:, 0:512],
                           XNT[:, cc * 2048 + j * 128: cc * 2048 + (j + 1) * 128],
                           WSTR[:, cc * 1024 + blk * 512: cc * 1024 + (blk + 1) * 512],
                           cc == 0, cc == 7,
                           r=[("tt", cc), ("wstr", cc)], w=[("ps", b)])
                    vcp(VTOK[:, j * 1024 + blk * 512: j * 1024 + (blk + 1) * 512],
                        PS[b][:, 0:512], r=[("ps", b)], w=[("vtok", j, blk)])

            if DEBUG_SPILLS and s == 1:
                dma("sync", d_dbgtt, XNT[:, 0:16384],
                    r=[("tt", c) for c in range(NCH)], out_sem=sOUT)
                dma("sync", d_dbgq, QT[:, 0:16384],
                    r=[("qt", m, t4) for m in range(NCH) for t4 in range(4)],
                    out_sem=sOUT)
                dma("sync", d_dbgk, KT[:, 0:16384],
                    r=[("kt", m, t4) for m in range(NCH) for t4 in range(4)],
                    out_sem=sOUT)
                dma("sync", d_dbgv, VTOK[:, 0:16384],
                    r=[("vtok", j, blk) for j in range(NT) for blk in range(2)],
                    out_sem=sOUT)
                n_out += 4

            S.fence()
            A.release()
            AOT = XNT
            A.mark()
            NSL = 4
            SC = A.t(F32, 2048 * NSL, "scb")
            PTS = A.t(BF16, 2048 * 2, "pts")
            ASM = A.t(F32, 16 * NSL, "asm")

            NITER = NT * NH

            def at_meta(it):
                i, h = it // NH, it % NH
                return i, h, (it % NSL) * 2048, it % NSL

            def at_A(it):
                i, h, scv_off, sl = at_meta(it)
                c = h // 2
                pb = (h % 2) * 64
                L = (i + 1) * 128
                nblk = (L + 511) // 512
                for b5 in range(nblk):
                    k0 = b5 * 512
                    wdt = min(512, L - k0)
                    b = (0, 1, 7)[(it * 4 + b5) % 3]
                    ktk = [("kt", c, tbb) for tbb in
                           range(k0 // 512, (k0 + wdt - 1) // 512 + 1)]
                    mm(PS[b][:, 0:wdt],
                       QT[pb:pb + 64, c * 2048 + i * 128: c * 2048 + (i + 1) * 128],
                       KT[pb:pb + 64, c * 2048 + k0: c * 2048 + k0 + wdt],
                       True, True, r=[("qt", c, i // 4)] + ktk, w=[("ps", b)])
                    S.op("scalar", lambda e, o=SC[:, scv_off + k0: scv_off + k0 + wdt],
                         i2=PS[b][:, 0:wdt]: e.copy(o, i2),
                         r=[("ps", b)], w=[("sc", sl)])

            def at_B(it):
                i, h, scv_off, sl = at_meta(it)
                L = (i + 1) * 128
                vtt(ALU.add, SC[:, scv_off + L - 128: scv_off + L],
                    SC[:, scv_off + L - 128: scv_off + L], TRI[:, :],
                    r=[("sc", sl), ("tri",)], w=[("sc", sl)])
                rmax(ASM[:, sl * 8: sl * 8 + 1], SC[:, scv_off: scv_off + L],
                     r=[("sc", sl)], w=[("m1", sl)])
                vts(ASM[:, sl * 8 + 1: sl * 8 + 2], ASM[:, sl * 8: sl * 8 + 1],
                    -0.125, None, ALU.mult, r=[("m1", sl)], w=[("nm", sl)])

            def at_C(it):
                i, h, scv_off, sl = at_meta(it)
                L = (i + 1) * 128
                act(SC[:, scv_off: scv_off + L], SC[:, scv_off: scv_off + L],
                    AF.Exp, bias=ASM[:, sl * 8 + 1: sl * 8 + 2], scale=0.125,
                    accum=ASM[:, sl * 8 + 2: sl * 8 + 3],
                    r=[("sc", sl), ("nm", sl)], w=[("sc", sl), ("rs", sl)])

            def at_D(it):
                i, h, scv_off, sl = at_meta(it)
                c = h // 2
                pb = (h % 2) * 64
                L = (i + 1) * 128
                psl = it % 2
                vrec(ASM[:, sl * 8 + 3: sl * 8 + 4], ASM[:, sl * 8 + 2: sl * 8 + 3],
                     r=[("rs", sl)], w=[("ri", sl)])
                vts(SC[:, scv_off: scv_off + L], SC[:, scv_off: scv_off + L],
                    ASM[:, sl * 8 + 3: sl * 8 + 4], None, ALU.mult,
                    r=[("sc", sl), ("ri", sl)], w=[("sc", sl)])
                ngrp = (i + 1 + 3) // 4
                for g4 in range(ngrp):
                    b = (2, 3, 6)[(it * 4 + g4) % 3]
                    nk = min(4, i + 1 - g4 * 4)
                    for q4 in range(nk):
                        kc = g4 * 4 + q4
                        trans(PS[b][:, q4 * 128:(q4 + 1) * 128],
                              SC[:, scv_off + kc * 128: scv_off + (kc + 1) * 128],
                              IDF[:, :], r=[("sc", sl), ("idf",)], w=[("ps", b)])
                    vcp(PTS[:, psl * 2048 + g4 * 512: psl * 2048 + g4 * 512 + nk * 128],
                        PS[b][:, 0: nk * 128], r=[("ps", b)], w=[("pts", psl, g4)])
                bao = 4 + (h % 2)
                for kc in range(i + 1):
                    mm(PS[bao][pb:pb + 64, 0:128],
                       VTOK[:, kc * 1024 + h * 64: kc * 1024 + h * 64 + 64],
                       PTS[:, psl * 2048 + kc * 128: psl * 2048 + (kc + 1) * 128],
                       kc == 0, kc == i, tp=(0, pb),
                       r=[("vtok", kc, h // 8), ("pts", psl, kc // 4)], w=[("ps", bao)])
                vcp(AOT[pb:pb + 64, c * 2048 + i * 128: c * 2048 + (i + 1) * 128],
                    PS[bao][pb:pb + 64, 0:128], r=[("ps", bao)],
                    w=[("aot", c, i, h % 2)])

            for t in range(NITER + 3):
                if t < NITER:
                    at_A(t)
                if 1 <= t < NITER + 1:
                    at_B(t - 1)
                if 2 <= t < NITER + 2:
                    at_C(t - 2)
                if 3 <= t < NITER + 3:
                    at_D(t - 3)

            if DEBUG_SPILLS and s == 1:
                dma("sync", d_dbgao, AOT[:, 0:16384],
                    r=[("aot", c, i, p) for c in range(NCH) for i in range(NT)
                       for p in range(2)],
                    out_sem=sOUT)
                n_out += 1
            S.fence()
            A.release()
            A.release()

            A.mark()
            WSTR2 = A.t(BF16, 8192, "wstr2")
            XNR = A.t(F32, 2048, "xnr")
            HSTG = A.t(F32, 2048, "hstg")
            for cc in range(NCH):
                dma("gpsimd", WSTR2[:, cc * 1024:(cc + 1) * 1024],
                    d_wot[cc * 128:(cc + 1) * 128, :], w=[("wo", cc)])
            for j in range(NT):
                sl = j % 2
                dma("sync", XNR[:, sl * 1024:(sl + 1) * 1024],
                    d_xnsp[j * 128:(j + 1) * 128, :],
                    r=[("xnsp", j)], w=[("xnr", sl)])
                for blk in range(2):
                    b = (j * 2 + blk) % 2
                    for cc in range(NCH):
                        mm(PS[b][:, 0:512],
                           AOT[:, cc * 2048 + j * 128: cc * 2048 + (j + 1) * 128],
                           WSTR2[:, cc * 1024 + blk * 512: cc * 1024 + (blk + 1) * 512],
                           cc == 0, cc == 7,
                           r=[("aot", cc, j, 0), ("aot", cc, j, 1), ("wo", cc)],
                           w=[("ps", b)])
                    vtt(ALU.add, HSTG[:, sl * 1024 + blk * 512: sl * 1024 + (blk + 1) * 512],
                        PS[b][:, 0:512], XNR[:, sl * 1024 + blk * 512: sl * 1024 + (blk + 1) * 512],
                        r=[("ps", b), ("xnr", sl)], w=[("hstg", sl, blk)])
                dma("scalar", d_hsp[j * 128:(j + 1) * 128, :],
                    HSTG[:, sl * 1024:(sl + 1) * 1024],
                    r=[("hstg", sl, 0), ("hstg", sl, 1)], w=[("hsp", j)])
            S.fence()
            A.release()
            A.release()

            A.mark()
            MOEW = A.t(BF16, D, "moew")
            A.mark()
            GROW2 = A.t(F32, D, "grow2")
            dma("sync", GROW2[0:1, :], d_moew, w=[("grow2",)])
            for nb in range(2):
                mm(PS[0][:, 0:512], ONESF[0:1, :],
                   GROW2[0:1, nb * 512:(nb + 1) * 512], True, True,
                   r=[("onesf",), ("grow2",)], w=[("ps", 0)])
                vcp(MOEW[:, nb * 512:(nb + 1) * 512], PS[0][:, 0:512],
                    r=[("ps", 0)], w=[("moeww",)])
            S.fence()
            A.release()

            HID = A.t(BF16, 32768, "hid")
            TNT = A.t(BF16, 8192, "tnt")
            W1B = A.t(BF16, 4096, "w1b")
            W2B = A.t(BF16, 2048, "w2b")
            B2E = A.t(BF16, 1024, "b2e")
            MOE = A.t(F32, 8192, "moe")
            HTS = A.t(F32, 2048, "hts")
            TNF = A.t(F32, 2048, "tnf")
            LOGW = A.t(F32, 128, "logw")
            GW = A.t(F32, 64, "gw")
            MSM = A.t(F32, 32, "msm")
            SST2 = A.t(F32, 2048, "sst2")
            XST2 = A.t(F32, 2048, "xst2")
            OUTS = A.t(F32, 2048, "outs")

            for tb in range(2):
                j0 = tb * 8
                vmemset(MOE[:, 0:8192], 0.0,
                        w=[("moe", jj, g) for jj in range(8) for g in range(2)])
                def t_T0(jj):
                    j = j0 + jj
                    sl = jj % 2
                    mb = sl * 16
                    lb = sl * 64
                    hv = HTS[:, sl * 1024:(sl + 1) * 1024]
                    dma("sync", hv, d_hsp[j * 128:(j + 1) * 128, :],
                        r=[("hsp", j)], w=[("hts", sl)])
                    vbn(LOGW[:, lb + 40: lb + 46], HTS[:, sl * 1024: sl * 1024 + 512],
                        r=[("hts", sl)], w=[("bn3", sl)])
                    vbn(LOGW[:, lb + 46: lb + 52], HTS[:, sl * 1024 + 512: (sl + 1) * 1024],
                        r=[("hts", sl)], w=[("bn3", sl)])
                    vbnagg(MSM[:, mb: mb + 2], LOGW[:, lb + 40: lb + 52],
                           r=[("bn3", sl)], w=[("mv3", sl)])
                    vtt(ALU.mult, MSM[:, mb + 2: mb + 3], MSM[:, mb: mb + 1],
                        MSM[:, mb: mb + 1], r=[("mv3", sl)], w=[("m3a", sl)])
                    vtt(ALU.add, MSM[:, mb + 3: mb + 4], MSM[:, mb + 2: mb + 3],
                        MSM[:, mb + 1: mb + 2], r=[("m3a", sl), ("mv3", sl)], w=[("m3b", sl)])
                    vts(MSM[:, mb + 4: mb + 5], MSM[:, mb + 3: mb + 4], 1e-6, None,
                        ALU.add, r=[("m3b", sl)], w=[("m3ba", sl)])
                    act(MSM[:, mb + 5: mb + 6], MSM[:, mb + 4: mb + 5], AF.Sqrt,
                        r=[("m3ba", sl)], w=[("m3c", sl)])
                    vrec(MSM[:, mb + 6: mb + 7], MSM[:, mb + 5: mb + 6],
                         r=[("m3c", sl)], w=[("m3d", sl)])

                def t_T1(jj):
                    sl = jj % 2
                    mb = sl * 16
                    hv = HTS[:, sl * 1024:(sl + 1) * 1024]
                    tnfv = TNF[:, sl * 1024:(sl + 1) * 1024]
                    vstt(tnfv, hv, MSM[:, mb + 6: mb + 7], MOEW[:, 0:1024],
                         ALU.mult, ALU.mult,
                         r=[("hts", sl), ("m3d", sl), ("moeww",)], w=[("tnf", sl)])
                    for cc in range(NCH):
                        b = 2 + cc % 2
                        trans(PS[b][:, 0:128], TNF[:, sl * 1024 + cc * 128: sl * 1024 + (cc + 1) * 128],
                              IDF[:, :], r=[("tnf", sl), ("idf",)], w=[("ps", b)])
                        vcp(TNT[:, cc * 1024 + jj * 128: cc * 1024 + (jj + 1) * 128],
                            PS[b][:, 0:128], r=[("ps", b)], w=[("tnt", cc, jj)])

                def t_T2(jj):
                    sl = jj % 2
                    mb = sl * 16
                    lb = sl * 64
                    bg = jj % 2
                    for cc in range(NCH):
                        mm(PS[bg][:, 0:8],
                           TNT[:, cc * 1024 + jj * 128: cc * 1024 + (jj + 1) * 128],
                           GATET[:, cc * 8:(cc + 1) * 8], cc == 0, cc == 7,
                           r=[("tnt", cc, jj), ("gatet",)], w=[("ps", bg)])
                    vcp(LOGW[:, lb: lb + 8], PS[bg][:, 0:8], r=[("ps", bg)], w=[("log", sl)])
                    rmax(MSM[:, mb + 8: mb + 9], LOGW[:, lb: lb + 8],
                         r=[("log", sl)], w=[("g1", sl)])
                    vts(LOGW[:, lb + 8: lb + 16], LOGW[:, lb: lb + 8],
                        MSM[:, mb + 8: mb + 9], None, ALU.is_equal,
                        r=[("log", sl), ("g1", sl)], w=[("mk1", sl)])
                    vstt(LOGW[:, lb + 16: lb + 24], LOGW[:, lb + 8: lb + 16], -1e30,
                         LOGW[:, lb: lb + 8], ALU.mult, ALU.add,
                         r=[("mk1", sl), ("log", sl)], w=[("l2", sl)])
                    rmax(MSM[:, mb + 9: mb + 10], LOGW[:, lb + 16: lb + 24],
                         r=[("l2", sl)], w=[("g2", sl)])
                    vts(LOGW[:, lb + 24: lb + 32], LOGW[:, lb + 16: lb + 24],
                        MSM[:, mb + 9: mb + 10], None, ALU.is_equal,
                        r=[("l2", sl), ("g2", sl)], w=[("mk2", sl)])
                    vts(MSM[:, mb + 10: mb + 11], MSM[:, mb + 8: mb + 9], -1.0, None,
                        ALU.mult, r=[("g1", sl)], w=[("ng1", sl)])
                    act(MSM[:, mb + 11: mb + 12], MSM[:, mb + 9: mb + 10], AF.Exp,
                        bias=MSM[:, mb + 10: mb + 11],
                        r=[("g2", sl), ("ng1", sl)], w=[("e2", sl)])
                    vts(MSM[:, mb + 12: mb + 13], MSM[:, mb + 11: mb + 12], 1.0, None,
                        ALU.add, r=[("e2", sl)], w=[("den", sl)])
                    vrec(MSM[:, mb + 13: mb + 14], MSM[:, mb + 12: mb + 13],
                         r=[("den", sl)], w=[("rd", sl)])
                    vstt(LOGW[:, lb + 32: lb + 40], LOGW[:, lb + 24: lb + 32],
                         MSM[:, mb + 11: mb + 12], LOGW[:, lb + 8: lb + 16],
                         ALU.mult, ALU.add,
                         r=[("mk2", sl), ("e2", sl), ("mk1", sl)], w=[("wn", sl)])
                    vts(GW[:, jj * 8:(jj + 1) * 8], LOGW[:, lb + 32: lb + 40],
                        MSM[:, mb + 13: mb + 14], None, ALU.mult,
                        r=[("wn", sl), ("rd", sl)], w=[("gw", jj)])

                for t_ in range(10):
                    if t_ < 8:
                        t_T0(t_)
                    if 1 <= t_ < 9:
                        t_T1(t_ - 1)
                    if 2 <= t_:
                        t_T2(t_ - 2)

                for e_ in range(NE):
                    for pr in range(16):
                        sb = pr % 2
                        r0 = (e_ * 32 + pr * 2) * 128
                        dma("gpsimd",
                            W1B[:, sb * 2048:(sb + 1) * 2048].rearrange(
                                "p (two d) -> p two d", two=2),
                            d_w1d[r0:r0 + 256, :].rearrange(
                                "(two p) d -> p two d", p=128),
                            w=[("w1b", sb)])
                        for sub in range(2):
                            b_ = pr * 2 + sub
                            for tsub in range(2):
                                hb = (b_ * 2 + tsub) % 4
                                for cc in range(NCH):
                                    mm(PS[hb][:, 0:512],
                                       W1B[:, sb * 2048 + sub * 1024 + cc * 128:
                                           sb * 2048 + sub * 1024 + (cc + 1) * 128],
                                       TNT[:, cc * 1024 + tsub * 512: cc * 1024 + (tsub + 1) * 512],
                                       cc == 0, cc == 7,
                                       r=[("w1b", sb)] + [("tnt", cc, t4)
                                                          for t4 in range(tsub * 4, tsub * 4 + 4)],
                                       w=[("ps", hb)])
                                act(HID[:, b_ * 1024 + tsub * 512: b_ * 1024 + (tsub + 1) * 512],
                                    PS[hb][:, 0:512], AF.Gelu,
                                    bias=B1T[:, e_ * 32 + b_: e_ * 32 + b_ + 1],
                                    r=[("ps", hb), ("b1t",)], w=[("hid", b_, tsub)])
                    dma("gpsimd", B2E[0:1, 0:1024], d_b2[e_:e_ + 1, :], w=[("b2e",)])
                    for g in range(2):
                        for kp in range(16):
                            sw = kp % 2
                            r0 = (((e_ * 2 + g) * 32 + kp * 2)) * 128
                            dma("gpsimd",
                                W2B[:, sw * 1024:(sw + 1) * 1024].rearrange(
                                    "p (two d) -> p two d", two=2),
                                d_w2d[r0:r0 + 256, :].rearrange(
                                    "(two p) d -> p two d", p=128),
                                w=[("w2b", sw)])
                            for sub in range(2):
                                k = kp * 2 + sub
                                for jj in range(8):
                                    mm(PS[jj][:, 0:512],
                                       HID[:, k * 1024 + jj * 128: k * 1024 + (jj + 1) * 128],
                                       W2B[:, sw * 1024 + sub * 512: sw * 1024 + (sub + 1) * 512],
                                       k == 0, False,
                                       r=[("hid", k, jj // 4), ("w2b", sw)],
                                       w=[("ps", jj)])
                        for jj in range(8):
                            mm(PS[jj][:, 0:512], ONESB[0:1, 0:128],
                               B2E[0:1, g * 512:(g + 1) * 512], False, True,
                               r=[("onesb",), ("b2e",)], w=[("ps", jj)])
                            vstt(MOE[:, jj * 1024 + g * 512: jj * 1024 + (g + 1) * 512],
                                 PS[jj][:, 0:512], GW[:, jj * 8 + e_: jj * 8 + e_ + 1],
                                 MOE[:, jj * 1024 + g * 512: jj * 1024 + (g + 1) * 512],
                                 ALU.mult, ALU.add,
                                 r=[("ps", jj), ("gw", jj), ("moe", jj, g)],
                                 w=[("moe", jj, g)])

                if DEBUG_SPILLS and s == 1 and tb == 1:
                    dma("sync", d_dbgtn, TNT[:, 0:8192],
                        r=[("tnt", cc, jj) for cc in range(NCH) for jj in range(8)],
                        out_sem=sOUT)
                    dma("sync", d_dbggw, GW[:, 0:64],
                        r=[("gw", jj) for jj in range(8)], out_sem=sOUT)
                    for jj in range(8):
                        dma("sync", d_dbgmoe[jj * 128:(jj + 1) * 128, :],
                            MOE[:, jj * 1024:(jj + 1) * 1024],
                            r=[("moe", jj, 0), ("moe", jj, 1)], out_sem=sOUT)
                    dma("sync", d_dbghid, HID[:, 0:32768],
                        r=[("hid", b_, t_) for b_ in range(32) for t_ in range(2)],
                        out_sem=sOUT)
                    n_out += 11

                def c_A(jj):
                    j = j0 + jj
                    sl = jj % 2
                    hv = HTS[:, sl * 1024:(sl + 1) * 1024]
                    dma("sync", hv, d_hsp[j * 128:(j + 1) * 128, :],
                        r=[("hsp", j)], w=[("hts", sl)])
                    fv = MOE[:, jj * 1024:(jj + 1) * 1024]
                    vtt(ALU.add, fv, hv, fv,
                        r=[("hts", sl), ("moe", jj, 0), ("moe", jj, 1)],
                        w=[("moe", jj, 0), ("moe", jj, 1)])
                    act(fv, fv, AF.Silu, r=[("moe", jj, 0)],
                        w=[("moe", jj, 0), ("moe", jj, 1)])
                    sv = SST2[:, sl * 1024:(sl + 1) * 1024]
                    dma("sync", sv, d_ssp[j * 128:(j + 1) * 128, :],
                        r=[("ssp", j)], w=[("sstr", sl)])
                    xv = XST2[:, sl * 1024:(sl + 1) * 1024]
                    dma("sync", xv, d_x[s * SEQ + j * 128: s * SEQ + (j + 1) * 128, :],
                        w=[("xstr", sl)])

                def c_B(jj):
                    j = j0 + jj
                    sl = jj % 2
                    fv = MOE[:, jj * 1024:(jj + 1) * 1024]
                    sv = SST2[:, sl * 1024:(sl + 1) * 1024]
                    xv = XST2[:, sl * 1024:(sl + 1) * 1024]
                    vtt(ALU.mult, fv, fv, sv,
                        r=[("moe", jj, 0), ("sstr", sl)],
                        w=[("moe", jj, 0), ("moe", jj, 1)])
                    ov = OUTS[:, sl * 1024:(sl + 1) * 1024]
                    vtt(ALU.add, ov, fv, xv,
                        r=[("moe", jj, 0), ("xstr", sl)], w=[("outs", sl)])
                    dma("sync", d_out[s * SEQ + j * 128: s * SEQ + (j + 1) * 128, :],
                        ov, r=[("outs", sl)], out_sem=sOUT)

                for t_ in range(9):
                    if t_ < 8:
                        c_A(t_)
                    if t_ >= 1:
                        c_B(t_ - 1)
                n_out += 8
            S.fence()
            A.release()

        stats = S.finalize_and_emit(block, final_waits=[(sOUT, n_out * 16)])
        total_ops = sum(v[0] for v in stats.values())
        print(f"[kernel] ops={stats} total={total_ops} arena_peak={A.peak}")
    return nc



def _host_tables():
    identf = np.eye(128, dtype=np.float32)
    tri = np.triu(np.full((128, 128), -1e30, np.float32), k=1)
    onesf = np.ones((1, 128), np.float32)
    onesb = np.ones((1, 128), BF)
    neg2 = np.concatenate([np.full((1, 16), -2 * LR, np.float32),
                           np.full((1, 16), -LR, np.float32)], axis=1)
    R = np.zeros((64, 64), np.float32)
    for i in range(32):
        R[i, i + 32] = -1.0
        R[i + 32, i] = 1.0
    R2 = np.zeros((128, 128), np.float32)
    R2[0:64, 0:64] = R
    R2[64:128, 64:128] = R
    r2t = R2.T.astype(BF)
    inv = (1.0 / (10000.0 ** (np.arange(0, 64, 2, dtype=np.float32) / 64.0)))
    ang = np.arange(SEQ, dtype=np.float32)[None, :] * inv[:, None]
    cos64 = np.concatenate([np.cos(ang), np.cos(ang)], axis=0)
    sin64 = np.concatenate([np.sin(ang), np.sin(ang)], axis=0)
    cosb = np.concatenate([cos64, cos64], axis=0).astype(BF)
    sinb = np.concatenate([sin64, sin64], axis=0).astype(BF)
    return identf, tri, onesf, onesb, neg2, r2t, cosb, sinb


def kernel(**inputs):
    x = np.asarray(inputs["L0_attn_x"], np.float32)
    Wb = np.ascontiguousarray(np.asarray(inputs["L0_attn_Wb"], np.float32))
    iA = np.asarray(inputs["L0_attn_init_A"], np.float32)
    iB = np.asarray(inputs["L0_attn_init_B"], np.float32)
    lng = np.asarray(inputs["L0_attn_ln_gamma"], np.float32)
    lnb = np.asarray(inputs["L0_attn_ln_beta"], np.float32)
    Wp = np.ascontiguousarray(np.asarray(inputs["L0_attn_Wp"], np.float32))
    anw = np.asarray(inputs["L0_ffn_attn_norm_w"], np.float32)
    wq = np.asarray(inputs["L0_ffn_wq"], np.float32)
    wk = np.asarray(inputs["L0_ffn_wk"], np.float32)
    wv = np.asarray(inputs["L0_ffn_wv"], np.float32)
    wo = np.asarray(inputs["L0_ffn_wo"], np.float32)
    mnw = np.asarray(inputs["L0_ffn_moe_norm_w"], np.float32)
    gw = np.asarray(inputs["L0_ffn_gate_w"], np.float32)
    w1 = np.asarray(inputs["L0_ffn_w1"], np.float32)
    b1 = np.asarray(inputs["L0_ffn_b1"], np.float32)
    w2 = np.asarray(inputs["L0_ffn_w2"], np.float32)
    b2 = np.asarray(inputs["L0_ffn_b2"], np.float32)

    identf, tri, onesf, onesb, neg2, r2t, cosb, sinb = _host_tables()

    w1d = np.ascontiguousarray(
        w1.reshape(8, 32, 128, 8, 128).transpose(0, 1, 4, 3, 2)).reshape(8 * 32 * 128, 1024)
    w2d = np.ascontiguousarray(
        w2.reshape(8, 2, 512, 32, 128).transpose(0, 1, 3, 4, 2)).reshape(8 * 2 * 32 * 128, 512)
    b1t = np.ascontiguousarray(b1.reshape(8, 32, 128).transpose(2, 0, 1)).reshape(128, 256)

    common = {
        "wbt": np.ascontiguousarray(Wb.T), "wpt": np.ascontiguousarray(Wp.T),
        "wpraw": Wp,
        "wqt": np.ascontiguousarray(wq.T), "wkt": np.ascontiguousarray(wk.T),
        "wvt": np.ascontiguousarray(wv.T), "wot": np.ascontiguousarray(wo.T),
        "w1d": w1d, "w2d": w2d, "b1t": b1t, "b2": np.ascontiguousarray(b2),
        "gatet": np.ascontiguousarray(gw.T),
        "initac": np.ascontiguousarray(iA),
        "initbtc": np.ascontiguousarray(iB.T),
        "initat": np.ascontiguousarray(iA.T),
        "initb": np.ascontiguousarray(iB),
        "lng": lng.reshape(1, D), "lnb": lnb.reshape(1, D),
        "moew": mnw.reshape(1, D),
        "attnwc": np.ascontiguousarray(anw.reshape(8, 128).T),
        "identf": identf, "tri": tri, "onesf": onesf, "onesb": onesb,
        "neg2": neg2, "r2t": r2t, "cosb": cosb, "sinb": sinb,
    }
    in_maps = []
    for core in range(8):
        m = dict(common)
        m["x"] = np.ascontiguousarray(x[2 * core:2 * core + 2].reshape(2 * SEQ, D))
        in_maps.append(m)

    nc = build_program()
    res = run_bass_kernel_spmd(nc, in_maps, core_ids=list(range(8)))
    outs = [np.asarray(r["out"], np.float32).reshape(2, SEQ, D) for r in res.results]
    return np.concatenate(outs, axis=0)


# version 09 (final)
# speedup vs baseline: 1.0918x; 1.0240x over previous
import numpy as np
import ml_dtypes
import concourse.bass as bass
import concourse.mybir as mybir
from concourse.bass_utils import run_bass_kernel_spmd

F32 = mybir.dt.float32
BF16 = mybir.dt.bfloat16
U8 = mybir.dt.uint8
AF = mybir.ActivationFunctionType
ALU = mybir.AluOpType
AX = mybir.AxisListType
BF = ml_dtypes.bfloat16

ENGINES = ("sync", "gpsimd", "tensor", "vector", "scalar")



class Op:
    __slots__ = ("eng", "fn", "dma", "need_inc", "cum", "deps", "pos", "inc_sem")

    def __init__(self, eng, fn, dma, pos):
        self.eng = eng
        self.fn = fn
        self.dma = dma
        self.need_inc = dma
        self.cum = None
        self.deps = []
        self.pos = pos
        self.inc_sem = None


class Sched:
    def __init__(self, eng_sems, dma_sems=None):
        self.eng_sems = eng_sems
        self.dma_sems = dma_sems or {}
        self.dma_cnt = {e: 0 for e in ENGINES}
        self.last_dma = {e: {} for e in ENGINES}
        self.ops = {e: [] for e in ENGINES}
        self.res = {}
        self.fence_ops = []

    def op(self, eng, fn, r=(), w=(), extra=(), dma=False, out_sem=None):
        o = Op(eng, fn, dma, len(self.ops[eng]))
        if dma and out_sem is None and eng in self.dma_sems:
            pool = self.dma_sems[eng]
            i = self.dma_cnt[eng] % len(pool)
            self.dma_cnt[eng] += 1
            o.inc_sem = pool[i]
            self.last_dma[eng][i] = o
        deps = list(extra) + list(self.fence_ops)
        for k in r:
            e = self.res.get(k)
            if e is not None and e[0] is not None:
                deps.append(e[0])
        for k in w:
            e = self.res.get(k)
            if e is not None:
                deps.extend(e[1])
                if e[0] is not None:
                    deps.append(e[0])
        best = {}
        for d in deps:
            if d is None or d is o:
                continue
            if d.eng == "tensor" and eng == "tensor":
                continue
            key = (d.eng, d.dma, id(d.inc_sem))
            b = best.get(key)
            if b is None or d.pos > b.pos:
                best[key] = d
        for d in best.values():
            d.need_inc = True
            o.deps.append(d)
        for k in r:
            self.res.setdefault(k, [None, []])[1].append(o)
        for k in w:
            self.res[k] = [o, []]
        if out_sem is not None:
            o.inc_sem = out_sem
            o.need_inc = True
        self.ops[eng].append(o)
        return o

    def fence(self):
        fo = []
        for e, ops in self.ops.items():
            if not ops:
                continue
            fo.append(ops[-1])
            fo.extend(self.last_dma[e].values())
        self.fence_ops = fo

    def finalize_and_emit(self, block, final_waits=()):
        cum = {}
        for e in ENGINES:
            for o in self.ops[e]:
                sem = o.inc_sem
                if sem is None:
                    sem = self.eng_sems[e]
                o.inc_sem = sem
                if o.need_inc:
                    cum[id(sem)] = cum.get(id(sem), 0) + (16 if o.dma else 1)
                o.cum = cum.get(id(sem), 0)
        stats = {e: (len(self.ops[e]), sum(1 for o in self.ops[e] if o.need_inc))
                 for e in ENGINES}
        for ename in ENGINES:
            def mk(name, olist):
                def _(eng):
                    lw = {}
                    for o in olist:
                        for d in o.deps:
                            if d.cum <= lw.get(id(d.inc_sem), 0):
                                continue
                            eng.wait_ge(d.inc_sem, d.cum)
                            lw[id(d.inc_sem)] = d.cum
                        inst = o.fn(eng)
                        if o.need_inc:
                            inst.then_inc(o.inc_sem, 16 if o.dma else 1)
                    if name == "sync":
                        for (sem, val) in final_waits:
                            eng.wait_ge(sem, val)
                return _
            getattr(block, ename)(mk(ename, self.ops[ename]))
        return stats


class View:
    __slots__ = ("ap", "off", "cols")

    def __init__(self, ap, off, cols):
        self.ap = ap
        self.off = off
        self.cols = cols

    def __getitem__(self, idx):
        p, c = idx
        if isinstance(c, slice):
            c0 = 0 if c.start is None else c.start
            c1 = self.cols if c.stop is None else c.stop
        else:
            c0, c1 = c, c + 1
        assert 0 <= c0 < c1 <= self.cols, f"view col range {c0}:{c1} of {self.cols}"
        return self.ap[p, c0:c1]


class Arena:

    def __init__(self, nc, handle, total):
        self.nc = nc
        self.base = nc.lookup_mloc(handle).addr
        self.total = total
        self.off = 0
        self.stack = []
        self.n = 0
        self.peak = 0

    def t(self, dtype, cols, tag=""):
        nbytes = cols * (4 if dtype == F32 else 2)
        nbytes = (nbytes + 31) & ~31
        assert self.off + nbytes <= self.total, \
            f"SBUF arena overflow at {tag}: {self.off + nbytes} > {self.total}"
        name = f"ar{self.n}_{tag}"
        self.n += 1
        hnd = self.nc.virtual_sbuf_tensor(name, [128, cols], dtype)
        self.nc._tensor(hnd.name, list(hnd.shape), hnd.dtype, type="SB")
        mloc = self.nc.lookup_mloc(hnd)
        mloc.allocated = True
        mloc.addr = self.base + self.off
        self.off += nbytes
        self.peak = max(self.peak, self.off)
        return View(hnd.ap(), 0, cols)

    def mark(self):
        self.stack.append(self.off)

    def release(self):
        self.off = self.stack.pop()



SEQ = 2048
D = 1024
NT = 16
NCH = 8
CH = 8
NH = 16
HD = 64
NE = 8
H4 = 4096
LR = 0.02
DEBUG_SPILLS = False


def build_program():
    nc = bass.Bass("TRN2", target_bir_lowering=False)

    def din(name, shape, dt=F32):
        return nc.dram_tensor(name, shape, dt, kind="ExternalInput").ap()

    d_x = din("x", [2 * SEQ, D])
    d_wbt = din("wbt", [D, D])
    d_wpt = din("wpt", [D, D])
    d_wpraw = din("wpraw", [D, D])
    d_wqt = din("wqt", [D, D])
    d_wkt = din("wkt", [D, D])
    d_wvt = din("wvt", [D, D])
    d_wot = din("wot", [D, D])
    d_w1d = din("w1d", [NE * 32 * 128, 1024])
    d_w2d = din("w2d", [NE * 2 * 32 * 128, 512])
    d_b1t = din("b1t", [128, 256])
    d_b2 = din("b2", [NE, D])
    d_gatet = din("gatet", [D, 8])
    d_initac = din("initac", [D, 16])
    d_initbtc = din("initbtc", [D, 16])
    d_initat = din("initat", [16, D])
    d_initb = din("initb", [16, D])
    d_lng = din("lng", [1, D])
    d_lnb = din("lnb", [1, D])
    d_moew = din("moew", [1, D])
    d_attnwc = din("attnwc", [128, 8])
    d_identf = din("identf", [128, 128])
    d_tri = din("tri", [128, 128])
    d_onesf = din("onesf", [1, 128])
    d_onesb = din("onesb", [1, 128], BF16)
    d_neg2 = din("neg2", [1, 32])
    d_r2t = din("r2t", [128, 128], BF16)
    d_cosb = din("cosb", [128, SEQ], BF16)
    d_sinb = din("sinb", [128, SEQ], BF16)

    d_out = nc.dram_tensor("out", [2 * SEQ, D], F32, kind="ExternalOutput").ap()
    kd = "ExternalOutput" if DEBUG_SPILLS else "Internal"
    d_r1sp = nc.dram_tensor("r1sp", [1, SEQ], F32, kind=kd).ap()
    d_xnsp = nc.dram_tensor("xnsp", [SEQ, D], F32, kind=kd).ap()
    d_xtmsp = nc.dram_tensor("xtmsp", [SEQ, D], BF16, kind=kd).ap()
    d_ssp = nc.dram_tensor("ssp", [SEQ, D], F32, kind=kd).ap()
    d_hsp = nc.dram_tensor("hsp", [SEQ, D], F32, kind=kd).ap()
    if DEBUG_SPILLS:
        d_dbgtt = nc.dram_tensor("dbgtt", [128, 16384], BF16, kind=kd).ap()
        d_dbgq = nc.dram_tensor("dbgq", [128, 16384], BF16, kind=kd).ap()
        d_dbgk = nc.dram_tensor("dbgk", [128, 16384], BF16, kind=kd).ap()
        d_dbgv = nc.dram_tensor("dbgv", [128, 16384], BF16, kind=kd).ap()
        d_dbgao = nc.dram_tensor("dbgao", [128, 16384], BF16, kind=kd).ap()
        d_dbgtn = nc.dram_tensor("dbgtn", [128, 8192], BF16, kind=kd).ap()
        d_dbggw = nc.dram_tensor("dbggw", [128, 64], F32, kind=kd).ap()
        d_dbgmoe = nc.dram_tensor("dbgmoe", [1024, D], F32, kind=kd).ap()
        d_dbghid = nc.dram_tensor("dbghid", [128, 32768], BF16, kind=kd).ap()

    ARENA_BYTES = 181248

    import contextlib
    with contextlib.ExitStack() as es:
        ARH = es.enter_context(nc.sbuf_tensor("arena", [128, ARENA_BYTES], U8))
        PS = [es.enter_context(nc.psum_tensor(f"ps{i}", [128, 512], F32))
              for i in range(8)]
        sSY = es.enter_context(nc.semaphore("sSY"))
        sGP = es.enter_context(nc.semaphore("sGP"))
        sPE = es.enter_context(nc.semaphore("sPE"))
        sDV = es.enter_context(nc.semaphore("sDV"))
        sAC = es.enter_context(nc.semaphore("sAC"))
        sOUT = es.enter_context(nc.semaphore("sOUT"))
        block = es.enter_context(nc.Block())
        A = Arena(nc, ARH, ARENA_BYTES)
        NPOOL = 8
        pool_sy = [es.enter_context(nc.semaphore(f"dsy{i}")) for i in range(NPOOL)]
        pool_gp = [es.enter_context(nc.semaphore(f"dgp{i}")) for i in range(NPOOL)]
        pool_st = [es.enter_context(nc.semaphore(f"dst{i}")) for i in range(NPOOL)]
        S = Sched({"sync": sSY, "gpsimd": sGP, "tensor": sPE,
                   "vector": sDV, "scalar": sAC},
                  dma_sems={"sync": pool_sy, "gpsimd": pool_gp,
                            "scalar": pool_st})

        def dma(eng, dst, src, r=(), w=(), out_sem=None):
            return S.op(eng, lambda e, d=dst, s=src: e.dma_start(out=d, in_=s),
                        r=r, w=w, dma=True, out_sem=out_sem)

        def mm(out, lhsT, rhs, start, stop, r=(), w=(), tp=None):
            return S.op("tensor",
                        lambda e, o=out, l=lhsT, rr=rhs, st=start, sp=stop, t=tp:
                        e.matmul(o, l, rr, start=st, stop=sp, tile_position=t),
                        r=r, w=w)

        def trans(out, in_, ident, r=(), w=()):
            return S.op("tensor",
                        lambda e, o=out, i=in_, d=ident: e.transpose(o, i, d),
                        r=r, w=w)

        def vcp(out, in_, r=(), w=()):
            return S.op("vector", lambda e, o=out, i=in_: e.tensor_copy(o, i),
                        r=r, w=w)

        def vtt(op, out, in0, in1, r=(), w=()):
            return S.op("vector",
                        lambda e, o=out, a=in0, b=in1, p=op: e.tensor_tensor(o, a, b, p),
                        r=r, w=w)

        def vts(out, in0, s1, s2, op0, op1=None, r=(), w=()):
            if op1 is None:
                return S.op("vector",
                            lambda e, o=out, a=in0, x=s1, p=op0:
                            e.tensor_scalar(o, a, x, None, p),
                            r=r, w=w)
            return S.op("vector",
                        lambda e, o=out, a=in0, x=s1, y=s2, p=op0, q=op1:
                        e.tensor_scalar(o, a, x, y, p, q),
                        r=r, w=w)

        def vstt(out, in0, scal, in1, op0, op1, r=(), w=()):
            return S.op("vector",
                        lambda e, o=out, a=in0, x=scal, b=in1, p=op0, q=op1:
                        e.scalar_tensor_tensor(o, a, x, b, p, q),
                        r=r, w=w)

        def act(out, in_, func, bias=None, scale=None, accum=None, r=(), w=()):
            def f(e, o=out, i=in_, fn=func, b=bias, sc=scale, a=accum):
                kw = {}
                if b is not None:
                    kw["bias"] = b
                if sc is not None:
                    kw["scale"] = sc
                if a is not None:
                    kw["accum_out"] = a
                return e.activation(o, i, fn, **kw)
            return S.op("scalar", f, r=r, w=w)

        def rmax(out, in_, r=(), w=()):
            return S.op("vector", lambda e, o=out, i=in_: e.reduce_max(o, i, axis=AX.X),
                        r=r, w=w)

        def rsum(out, in_, r=(), w=()):
            return S.op("vector", lambda e, o=out, i=in_: e.reduce_sum(o, i, axis=AX.X),
                        r=r, w=w)

        def vrec(out, in_, r=(), w=()):
            return S.op("vector", lambda e, o=out, i=in_: e.reciprocal(o, i), r=r, w=w)

        def vmemset(ap_, val, r=(), w=()):
            return S.op("vector", lambda e, a=ap_, v=val: e.memset(a, v), r=r, w=w)

        def vbn(out, in_, r=(), w=()):
            return S.op("vector", lambda e, o=out, i=in_: e.bn_stats(o, i), r=r, w=w)

        def vbnagg(out, in_, r=(), w=()):
            return S.op("vector", lambda e, o=out, i=in_: e.bn_aggr(o, i), r=r, w=w)

        def vttr(out, in0, in1, accum, r=(), w=()):
            return S.op("vector",
                        lambda e, o=out, a=in0, b=in1, ac=accum:
                        e.tensor_tensor_reduce(o, a, b, 1.0, 0.0, ALU.mult,
                                               ALU.add, ac),
                        r=r, w=w)

        IDF = A.t(F32, 128, "idf")
        TRI = A.t(F32, 128, "tri")
        ONESF = A.t(F32, 128, "onesf")
        ONESB = A.t(BF16, 128, "onesb")
        NEG2 = A.t(F32, 32, "neg2")
        ATTNW = A.t(F32, 16, "attnw")
        B1T = A.t(F32, 256, "b1t")
        R2T = A.t(BF16, 128, "r2t")
        GATET = A.t(BF16, 64, "gatet")
        WPTA = A.t(BF16, 128, "wpta")
        INITBT = A.t(BF16, 128, "initbt")

        dma("sync", IDF[:, :], d_identf, w=[("idf",)])
        dma("sync", TRI[:, :], d_tri, w=[("tri",)])
        dma("sync", ONESF[0:1, :], d_onesf, w=[("onesf",)])
        dma("sync", ONESB[0:1, :], d_onesb, w=[("onesb",)])
        dma("sync", NEG2[0:1, :], d_neg2, w=[("neg2",)])
        dma("sync", ATTNW[:, 0:8], d_attnwc, w=[("attnw",)])
        dma("sync", B1T[:, :], d_b1t, w=[("b1t",)])
        dma("sync", R2T[:, :], d_r2t, w=[("r2t",)])
        for c in range(NCH):
            dma("gpsimd", GATET[:, c * 8:(c + 1) * 8],
                d_gatet[c * 128:(c + 1) * 128, :], w=[("gatet",)])
            dma("gpsimd", INITBT[:, c * 16:(c + 1) * 16],
                d_initbtc[c * 128:(c + 1) * 128, :], w=[("initbt",)])

        A.mark()
        WPR = A.t(BF16, 8192, "wpr")
        INA = A.t(BF16, 128, "ina")
        for c in range(NCH):
            dma("gpsimd", WPR[:, c * 1024:(c + 1) * 1024],
                d_wpraw[c * 128:(c + 1) * 128, :], w=[("wpr", c)])
            dma("gpsimd", INA[:, c * 16:(c + 1) * 16],
                d_initac[c * 128:(c + 1) * 128, :], w=[("ina",)])
        for m in range(8):
            for hc in range(NCH):
                mm(PS[0][:, 0:16], WPR[:, hc * 1024 + m * 128: hc * 1024 + (m + 1) * 128],
                   INA[:, hc * 16:(hc + 1) * 16], hc == 0, hc == 7,
                   r=[("wpr", hc), ("ina",)], w=[("ps", 0)])
            vcp(WPTA[:, m * 16:(m + 1) * 16], PS[0][:, 0:16],
                r=[("ps", 0)], w=[("wpta",)])
        S.fence()
        A.release()

        n_out = 0

        for s in range(2):
            A.mark()
            XNT = A.t(BF16, 16384, "xnt")

            A.mark()
            SHT = A.t(BF16, 16384, "sht")
            GAMB = A.t(F32, D, "gamb")
            BETB = A.t(F32, D, "betb")
            GROW = A.t(F32, D, "grow")
            XST = A.t(F32, 2048, "xst")
            XNS = A.t(F32, 2048, "xns")
            X2A = A.t(F32, 2048, "x2a")
            X2B = A.t(F32, 2048, "x2b")
            XTMC = A.t(BF16, 2048, "xtmc")
            BN = A.t(F32, 32, "bn")
            MV = A.t(F32, 8, "mv")
            SMALL = A.t(F32, 32, "small")
            RSTD1C = A.t(F32, 16, "r1c")
            R1ROW = A.t(F32, SEQ, "r1row")

            for (drow, dst, tag) in ((d_lng, GAMB, "gamb"), (d_lnb, BETB, "betb")):
                dma("sync", GROW[0:1, :], drow, r=[("grow",)], w=[("grow",)])
                for nb in range(2):
                    mm(PS[0][:, 0:512], ONESF[0:1, :],
                       GROW[0:1, nb * 512:(nb + 1) * 512], True, True,
                       r=[("onesf",), ("grow",)], w=[("ps", 0)])
                    vcp(dst[:, nb * 512:(nb + 1) * 512], PS[0][:, 0:512],
                        r=[("ps", 0)], w=[(tag,)])

            for c in range(NCH):
                vmemset(SHT[:, c * 2048 + 2047: c * 2048 + 2048], 0.0,
                        w=[("sht", c, 16)])

            def p0_S0(j):
                sl = j % 2
                mb = sl * 16
                xv = XST[:, sl * 1024:(sl + 1) * 1024]
                dma("sync", xv, d_x[s * SEQ + j * 128: s * SEQ + (j + 1) * 128, :],
                    w=[("xst", sl)])
                vbn(BN[:, mb: mb + 6], XST[:, sl * 1024: sl * 1024 + 512],
                    r=[("xst", sl)], w=[("bn", sl)])
                vbn(BN[:, mb + 6: mb + 12], XST[:, sl * 1024 + 512: (sl + 1) * 1024],
                    r=[("xst", sl)], w=[("bn", sl)])
                vbnagg(MV[:, sl * 4: sl * 4 + 2], BN[:, mb: mb + 12],
                       r=[("bn", sl)], w=[("mv", sl)])
                vts(SMALL[:, mb + 8: mb + 9], MV[:, sl * 4 + 1: sl * 4 + 2], 1e-5,
                    None, ALU.add, r=[("mv", sl)], w=[("sq1a", sl)])
                act(SMALL[:, mb: mb + 1], SMALL[:, mb + 8: mb + 9], AF.Sqrt,
                    r=[("sq1a", sl)], w=[("sq1", sl)])
                vrec(SMALL[:, mb + 1: mb + 2], SMALL[:, mb: mb + 1],
                     r=[("sq1", sl)], w=[("rstd", sl)])

            def p0_S1(j):
                sl = j % 2
                mb = sl * 16
                xv = XST[:, sl * 1024:(sl + 1) * 1024]
                xnv = XNS[:, sl * 1024:(sl + 1) * 1024]
                vts(xnv, xv, MV[:, sl * 4: sl * 4 + 1], SMALL[:, mb + 1: mb + 2],
                    ALU.subtract, ALU.mult,
                    r=[("xst", sl), ("mv", sl), ("rstd", sl)], w=[("xns", sl)])
                dma("scalar", d_xnsp[j * 128:(j + 1) * 128, :], xnv,
                    r=[("xns", sl)], w=[("xnsp", j)])
                xtv = XTMC[:, sl * 1024:(sl + 1) * 1024]
                vcp(xtv, xnv, r=[("xns", sl)], w=[("xtmc", sl)])
                dma("scalar", d_xtmsp[j * 128:(j + 1) * 128, :], xtv,
                    r=[("xtmc", sl)], w=[("xtmsp", j)])
                for c in range(NCH):
                    b = (2, 3, 6)[(j * 8 + c) % 3]
                    trans(PS[b][:, 0:128], XNS[:, sl * 1024 + c * 128: sl * 1024 + (c + 1) * 128],
                          IDF[:, :], r=[("xns", sl), ("idf",)], w=[("ps", b)])
                    vcp(XNT[:, c * 2048 + j * 128: c * 2048 + (j + 1) * 128],
                        PS[b][:, 0:128], r=[("ps", b)], w=[("xnt", c, j)])
                vbn(BN[:, mb: mb + 6], XNS[:, sl * 1024: sl * 1024 + 512],
                    r=[("xns", sl)], w=[("bn2", sl)])
                vbn(BN[:, mb + 6: mb + 12], XNS[:, sl * 1024 + 512: (sl + 1) * 1024],
                    r=[("xns", sl)], w=[("bn2", sl)])
                vbnagg(MV[:, sl * 4 + 2: sl * 4 + 4], BN[:, mb: mb + 12],
                       r=[("bn2", sl)], w=[("mv2", sl)])
                vts(SMALL[:, mb + 9: mb + 10], MV[:, sl * 4 + 3: sl * 4 + 4], 1e-5,
                    None, ALU.add, r=[("mv2", sl)], w=[("sq2a", sl)])
                act(SMALL[:, mb + 2: mb + 3], SMALL[:, mb + 9: mb + 10], AF.Sqrt,
                    r=[("sq2a", sl)], w=[("sq2", sl)])
                vrec(SMALL[:, mb + 3: mb + 4], SMALL[:, mb + 2: mb + 3],
                     r=[("sq2", sl)], w=[("rstd2", sl)])
                vtt(ALU.mult, SMALL[:, mb + 4: mb + 5], MV[:, sl * 4 + 2: sl * 4 + 3],
                    MV[:, sl * 4 + 2: sl * 4 + 3], r=[("mv2", sl)], w=[("m2sq", sl)])
                vtt(ALU.add, SMALL[:, mb + 5: mb + 6], SMALL[:, mb + 4: mb + 5],
                    MV[:, sl * 4 + 3: sl * 4 + 4],
                    r=[("m2sq", sl), ("mv2", sl)], w=[("msq", sl)])
                vts(SMALL[:, mb + 10: mb + 11], SMALL[:, mb + 5: mb + 6], 1e-6,
                    None, ALU.add, r=[("msq", sl)], w=[("sq3a", sl)])
                act(SMALL[:, mb + 6: mb + 7], SMALL[:, mb + 10: mb + 11], AF.Sqrt,
                    r=[("sq3a", sl)], w=[("sq3", sl)])
                vrec(RSTD1C[:, j: j + 1], SMALL[:, mb + 6: mb + 7],
                     r=[("sq3", sl)], w=[("r1c", j)])

            def p0_S2(j):
                sl = j % 2
                mb = sl * 16
                xnv = XNS[:, sl * 1024:(sl + 1) * 1024]
                x2av = X2A[:, sl * 1024:(sl + 1) * 1024]
                x2bv = X2B[:, sl * 1024:(sl + 1) * 1024]
                vts(x2av, xnv, MV[:, sl * 4 + 2: sl * 4 + 3], SMALL[:, mb + 3: mb + 4],
                    ALU.subtract, ALU.mult,
                    r=[("xns", sl), ("mv2", sl), ("rstd2", sl)], w=[("x2a", sl)])
                vtt(ALU.mult, x2bv, x2av, GAMB[:, 0:1024],
                    r=[("x2a", sl), ("gamb",)], w=[("x2b", sl)])
                vtt(ALU.add, x2bv, x2bv, BETB[:, 0:1024],
                    r=[("x2b", sl), ("betb",)], w=[("x2b", sl)])
                for c in range(NCH):
                    b = (2, 3, 6)[(j * 8 + c) % 3]
                    trans(PS[b][:, 0:128], X2B[:, sl * 1024 + c * 128: sl * 1024 + (c + 1) * 128],
                          IDF[:, :], r=[("x2b", sl), ("idf",)], w=[("ps", b)])
                    if j == 0:
                        vcp(SHT[:, c * 2048 + 0: c * 2048 + 127],
                            PS[b][:, 1:128], r=[("ps", b)], w=[("sht", c, 0)])
                    else:
                        vcp(SHT[:, c * 2048 + j * 128 - 1: c * 2048 + j * 128 + 127],
                            PS[b][:, 0:128], r=[("ps", b)], w=[("sht", c, j)])

            for t_ in range(NT + 2):
                if t_ < NT:
                    p0_S0(t_)
                if 1 <= t_ < NT + 1:
                    p0_S1(t_ - 1)
                if 2 <= t_:
                    p0_S2(t_ - 2)

            for j in range(NT):
                trans(PS[4][0:1, 0:128], RSTD1C[:, j: j + 1], IDF[:, :],
                      r=[("r1c", j), ("idf",)], w=[("ps", 4)])
                vcp(R1ROW[0:1, j * 128:(j + 1) * 128], PS[4][0:1, 0:128],
                    r=[("ps", 4)], w=[("r1row",)])
            dma("scalar", d_r1sp, R1ROW[0:1, 0:SEQ], r=[("r1row",)], w=[("r1sp",)])

            S.fence()
            A.release()

            A.mark()
            SHTb = A.t(BF16, 16384, "sht2")
            WBT = A.t(BF16, 8192, "wbtv")
            WPT = A.t(BF16, 8192, "wptv")
            VCH = A.t(BF16, 4096, "vch")
            XTM = A.t(BF16, 4096, "xtm")
            SST = A.t(F32, 2048, "sst")
            PIN = A.t(BF16, 32, "pin")
            PERR = A.t(BF16, 32, "perr")
            MIDT = A.t(BF16, 256, "midt")
            AEF = A.t(BF16, D, "aeff")
            BEFFT = A.t(BF16, 128, "befft")
            IAT = A.t(F32, D, "iat")
            IAT2 = A.t(F32, D, "iat2")
            IB = A.t(F32, D, "ib")
            ACCA = A.t(F32, D, "acca")
            ACCB = A.t(F32, D, "accb")
            TSCR = A.t(F32, D, "tscr")
            BEFF = A.t(F32, D, "beff")
            SM = A.t(F32, 48, "sm")

            for c in range(NCH):
                dma("gpsimd", WBT[:, c * 1024:(c + 1) * 1024],
                    d_wbt[c * 128:(c + 1) * 128, :], w=[("wbt", c)])
                dma("gpsimd", WPT[:, c * 1024:(c + 1) * 1024],
                    d_wpt[c * 128:(c + 1) * 128, :], w=[("wpt", c)])
            dma("sync", IAT[0:16, :], d_initat, w=[("iat",)])
            dma("sync", IB[0:16, :], d_initb, w=[("ib",)])
            vts(IAT2[0:16, :], IAT[0:16, :], 2.0, None, ALU.mult,
                r=[("iat",)], w=[("iat2",)])
            vmemset(ACCA[0:16, :], 0.0, w=[("acca",)])
            vmemset(ACCB[0:16, :], 0.0, w=[("accb",)])

            for c in range(CH):
                vs = c % 2
                for tt in range(2):
                    jt = 2 * c + tt
                    dma("sync", XTM[:, (vs * 2 + tt) * 1024:(vs * 2 + tt + 1) * 1024],
                        d_xtmsp[jt * 128:(jt + 1) * 128, :],
                        r=[("xtmsp", jt)], w=[("xtm", vs, tt)])
                for tt in range(2):
                    jt = 2 * c + tt
                    for blk in range(2):
                        b = (tt * 2 + blk) % 2
                        for cc in range(NCH):
                            mm(PS[b][:, 0:512],
                               SHTb[:, cc * 2048 + jt * 128: cc * 2048 + (jt + 1) * 128],
                               WPT[:, cc * 1024 + blk * 512: cc * 1024 + (blk + 1) * 512],
                               cc == 0, cc == 7,
                               r=[("sht", cc, jt), ("sht", cc, jt + 1), ("wpt", cc)],
                               w=[("ps", b)])
                        vcp(VCH[:, (vs * 2 + tt) * 1024 + blk * 512:
                                (vs * 2 + tt) * 1024 + (blk + 1) * 512],
                            PS[b][:, 0:512], r=[("ps", b)], w=[("vch", vs, tt, blk)])

                vtt(ALU.mult, TSCR[0:16, :], ACCA[0:16, :], ACCA[0:16, :],
                    r=[("acca",)], w=[("tscr",)])
                rsum(SM[0:16, 0:1], TSCR[0:16, :], r=[("tscr",)], w=[("sa",)])
                vtt(ALU.mult, TSCR[0:16, :], ACCB[0:16, :], ACCB[0:16, :],
                    r=[("accb",), ("sa",)], w=[("tscr",)])
                rsum(SM[0:16, 1:2], TSCR[0:16, :], r=[("tscr",)], w=[("sb",)])
                vtt(ALU.add, SM[0:16, 2:3], SM[0:16, 0:1], SM[0:16, 1:2],
                    r=[("sa",), ("sb",)], w=[("sab",)])
                trans(PS[7][0:1, 0:16], SM[0:16, 2:3], IDF[0:16, 0:16],
                      r=[("sab",), ("idf",)], w=[("ps", 7)])
                vcp(SM[0:1, 16:32], PS[7][0:1, 0:16], r=[("ps", 7)], w=[("row16",)])
                rsum(SM[0:1, 8:9], SM[0:1, 16:32], r=[("row16",)], w=[("ssq",)])
                act(SM[0:1, 9:10], SM[0:1, 8:9], AF.Sqrt, scale=LR * LR,
                    r=[("ssq",)], w=[("tn",)])
                vts(SM[0:1, 10:11], SM[0:1, 9:10], 1e-6, None, ALU.add,
                    r=[("tn",)], w=[("tne",)])
                vrec(SM[0:1, 12:13], SM[0:1, 10:11], r=[("tne",)], w=[("rec",)])
                vts(SM[0:1, 11:12], SM[0:1, 12:13], 1.0, None, ALU.min,
                    r=[("rec",)], w=[("coef",)])
                mm(PS[7][0:16, 0:1], NEG2[0:1, 0:16], SM[0:1, 11:12], True, True,
                   r=[("neg2",), ("coef",)], w=[("ps", 7)])
                vcp(SM[0:16, 3:4], PS[7][0:16, 0:1], r=[("ps", 7)], w=[("c2a",)])
                mm(PS[7][0:16, 0:1], NEG2[0:1, 16:32], SM[0:1, 11:12], True, True,
                   r=[("neg2",), ("coef",)], w=[("ps", 7)])
                vcp(SM[0:16, 4:5], PS[7][0:16, 0:1], r=[("ps", 7)], w=[("c2b",)])
                vstt(AEF[0:16, :], ACCA[0:16, :], SM[0:16, 3:4], IAT2[0:16, :],
                     ALU.mult, ALU.add, r=[("acca",), ("c2a",), ("iat2",)], w=[("aef",)])
                vstt(BEFF[0:16, :], ACCB[0:16, :], SM[0:16, 4:5], IB[0:16, :],
                     ALU.mult, ALU.add, r=[("accb",), ("c2b",), ("ib",)], w=[("beff",)])
                for cc in range(NCH):
                    trans(PS[7][:, 0:16], BEFF[0:16, cc * 128:(cc + 1) * 128],
                          IDF[0:16, 0:16], r=[("beff",), ("idf",)], w=[("ps", 7)])
                    vcp(BEFFT[:, cc * 16:(cc + 1) * 16], PS[7][:, 0:16],
                        r=[("ps", 7)], w=[("befft", cc)])

                for tt in range(2):
                    jt = 2 * c + tt
                    for cc in range(NCH):
                        mm(PS[4][0:16, 0:128], BEFFT[:, cc * 16:(cc + 1) * 16],
                           XNT[:, cc * 2048 + jt * 128: cc * 2048 + (jt + 1) * 128],
                           cc == 0, cc == 7,
                           r=[("befft", cc), ("xnt", cc, jt)], w=[("ps", 4)])
                    vcp(MIDT[0:16, tt * 128:(tt + 1) * 128], PS[4][0:16, 0:128],
                        r=[("ps", 4)], w=[("midt", tt)])
                for tt in range(2):
                    jt = 2 * c + tt
                    ss = jt % 2
                    for blk in range(2):
                        b = 5 + blk
                        for cc in range(NCH):
                            mm(PS[b][:, 0:512],
                               XNT[:, cc * 2048 + jt * 128: cc * 2048 + (jt + 1) * 128],
                               WBT[:, cc * 1024 + blk * 512: cc * 1024 + (blk + 1) * 512],
                               cc == 0, False,
                               r=[("xnt", cc, jt), ("wbt", cc)], w=[("ps", b)])
                        mm(PS[b][:, 0:512], MIDT[0:16, tt * 128:(tt + 1) * 128],
                           AEF[0:16, blk * 512:(blk + 1) * 512], False, True,
                           r=[("midt", tt), ("aef",)], w=[("ps", b)])
                        vcp(SST[:, ss * 1024 + blk * 512: ss * 1024 + (blk + 1) * 512],
                            PS[b][:, 0:512], r=[("ps", b)], w=[("sst", ss, blk)])
                    dma("scalar", d_ssp[jt * 128:(jt + 1) * 128, :],
                        SST[:, ss * 1024:(ss + 1) * 1024],
                        r=[("sst", ss, 0), ("sst", ss, 1)], w=[("ssp", jt)])

                for tt in range(2):
                    jt = 2 * c + tt
                    b = tt % 2
                    for cc in range(NCH):
                        mm(PS[b][:, 0:16],
                           XNT[:, cc * 2048 + jt * 128: cc * 2048 + (jt + 1) * 128],
                           INITBT[:, cc * 16:(cc + 1) * 16], cc == 0, cc == 7,
                           r=[("xnt", cc, jt), ("initbt",)], w=[("ps", b)])
                    vcp(PIN[:, tt * 16:(tt + 1) * 16], PS[b][:, 0:16],
                        r=[("ps", b)], w=[("pin", tt)])
                    for cc in range(NCH):
                        mm(PS[2 + b][:, 0:16],
                           SHTb[:, cc * 2048 + jt * 128: cc * 2048 + (jt + 1) * 128],
                           WPTA[:, cc * 16:(cc + 1) * 16], cc == 0, cc == 7,
                           r=[("sht", cc, jt), ("sht", cc, jt + 1), ("wpta",)],
                           w=[("ps", 2 + b)])
                    vcp(PERR[:, tt * 16:(tt + 1) * 16], PS[2 + b][:, 0:16],
                        r=[("ps", 2 + b)], w=[("perr", tt)])

                for nb in range(2):
                    b = 2 + nb
                    for tt in range(2):
                        mm(PS[b][0:16, 0:512], PIN[:, tt * 16:(tt + 1) * 16],
                           VCH[:, (vs * 2 + tt) * 1024 + nb * 512:
                               (vs * 2 + tt) * 1024 + (nb + 1) * 512],
                           tt == 0, tt == 1,
                           r=[("pin", tt), ("vch", vs, tt, nb)], w=[("ps", b)])
                    vtt(ALU.add, ACCA[0:16, nb * 512:(nb + 1) * 512],
                        ACCA[0:16, nb * 512:(nb + 1) * 512], PS[b][0:16, 0:512],
                        r=[("acca",), ("ps", b)], w=[("acca",)])
                for nb in range(2):
                    b = 2 + nb
                    for tt in range(2):
                        mm(PS[b][0:16, 0:512], PERR[:, tt * 16:(tt + 1) * 16],
                           XTM[:, (vs * 2 + tt) * 1024 + nb * 512:
                               (vs * 2 + tt) * 1024 + (nb + 1) * 512],
                           tt == 0, tt == 1,
                           r=[("perr", tt), ("xtm", vs, tt)], w=[("ps", b)])
                    vtt(ALU.add, ACCB[0:16, nb * 512:(nb + 1) * 512],
                        ACCB[0:16, nb * 512:(nb + 1) * 512], PS[b][0:16, 0:512],
                        r=[("accb",), ("ps", b)], w=[("accb",)])

            S.fence()
            A.release()

            A.mark()
            A.mark()
            R1B = A.t(F32, SEQ, "r1b")
            R1R2 = A.t(F32, SEQ, "r1r2")
            dma("sync", R1R2[0:1, 0:SEQ], d_r1sp, r=[("r1sp",)], w=[("r1r2",)])
            for nb in range(4):
                mm(PS[5][:, 0:512], ONESF[0:1, :],
                   R1R2[0:1, nb * 512:(nb + 1) * 512], True, True,
                   r=[("onesf",), ("r1r2",)], w=[("ps", 5)])
                vcp(R1B[:, nb * 512:(nb + 1) * 512], PS[5][:, 0:512],
                    r=[("ps", 5)], w=[("r1b",)])
            for c in range(NCH):
                vstt(XNT[:, c * 2048:(c + 1) * 2048], XNT[:, c * 2048:(c + 1) * 2048],
                     ATTNW[:, c: c + 1], R1B[:, 0:SEQ], ALU.mult, ALU.mult,
                     r=[("xnt", c, j) for j in range(NT)] + [("attnw",), ("r1b",)],
                     w=[("tt", c)])
            S.fence()
            A.release()

            QT = A.t(BF16, 16384, "qt")
            KT = A.t(BF16, 16384, "kt")
            VTOK = A.t(BF16, 16384, "vtok")
            A.mark()
            WSTR = A.t(BF16, 8192, "wstr")
            QRAW = A.t(BF16, 1024, "qraw")
            RT1 = A.t(F32, 1024, "rt1")
            RT2 = A.t(F32, 1024, "rt2")
            COSB = A.t(BF16, SEQ, "cosb")
            SINB = A.t(BF16, SEQ, "sinb")
            dma("sync", COSB[:, :], d_cosb, w=[("cosb",)])
            dma("sync", SINB[:, :], d_sinb, w=[("sinb",)])

            for (dw, dstv, tag) in ((d_wqt, QT, "qt"), (d_wkt, KT, "kt")):
                for cc in range(NCH):
                    dma("gpsimd", WSTR[:, cc * 1024:(cc + 1) * 1024],
                        dw[cc * 128:(cc + 1) * 128, :], w=[("wstr", cc)])
                for m in range(NCH):
                    for tb in range(4):
                        bq = m % 2
                        for cc in range(NCH):
                            mm(PS[bq][:, 0:512],
                               WSTR[:, cc * 1024 + m * 128: cc * 1024 + (m + 1) * 128],
                               XNT[:, cc * 2048 + tb * 512: cc * 2048 + (tb + 1) * 512],
                               cc == 0, cc == 7,
                               r=[("wstr", cc), ("tt", cc)], w=[("ps", bq)])
                        qs = (m * 4 + tb) % 2
                        vcp(QRAW[:, qs * 512:(qs + 1) * 512], PS[bq][:, 0:512],
                            r=[("ps", bq)], w=[("qraw", qs)])
                        br = 2 + qs
                        mm(PS[br][:, 0:512], R2T[:, 0:128], QRAW[:, qs * 512:(qs + 1) * 512],
                           True, True, r=[("r2t",), ("qraw", qs)], w=[("ps", br)])
                        vtt(ALU.mult, RT1[:, qs * 512:(qs + 1) * 512],
                            QRAW[:, qs * 512:(qs + 1) * 512],
                            COSB[:, tb * 512:(tb + 1) * 512],
                            r=[("qraw", qs), ("cosb",)], w=[("rt1", qs)])
                        vtt(ALU.mult, RT2[:, qs * 512:(qs + 1) * 512],
                            PS[br][:, 0:512], SINB[:, tb * 512:(tb + 1) * 512],
                            r=[("ps", br), ("sinb",)], w=[("rt2", qs)])
                        vtt(ALU.add, dstv[:, m * 2048 + tb * 512: m * 2048 + (tb + 1) * 512],
                            RT1[:, qs * 512:(qs + 1) * 512], RT2[:, qs * 512:(qs + 1) * 512],
                            r=[("rt1", qs), ("rt2", qs)], w=[(tag, m, tb)])
            for cc in range(NCH):
                dma("gpsimd", WSTR[:, cc * 1024:(cc + 1) * 1024],
                    d_wvt[cc * 128:(cc + 1) * 128, :], w=[("wstr", cc)])
            for j in range(NT):
                for blk in range(2):
                    b = (j * 2 + blk) % 2
                    for cc in range(NCH):
                        mm(PS[b][:, 0:512],
                           XNT[:, cc * 2048 + j * 128: cc * 2048 + (j + 1) * 128],
                           WSTR[:, cc * 1024 + blk * 512: cc * 1024 + (blk + 1) * 512],
                           cc == 0, cc == 7,
                           r=[("tt", cc), ("wstr", cc)], w=[("ps", b)])
                    vcp(VTOK[:, j * 1024 + blk * 512: j * 1024 + (blk + 1) * 512],
                        PS[b][:, 0:512], r=[("ps", b)], w=[("vtok", j, blk)])

            if DEBUG_SPILLS and s == 1:
                dma("sync", d_dbgtt, XNT[:, 0:16384],
                    r=[("tt", c) for c in range(NCH)], out_sem=sOUT)
                dma("sync", d_dbgq, QT[:, 0:16384],
                    r=[("qt", m, t4) for m in range(NCH) for t4 in range(4)],
                    out_sem=sOUT)
                dma("sync", d_dbgk, KT[:, 0:16384],
                    r=[("kt", m, t4) for m in range(NCH) for t4 in range(4)],
                    out_sem=sOUT)
                dma("sync", d_dbgv, VTOK[:, 0:16384],
                    r=[("vtok", j, blk) for j in range(NT) for blk in range(2)],
                    out_sem=sOUT)
                n_out += 4

            S.fence()
            A.release()
            AOT = XNT
            A.mark()
            NSL = 4
            SC = A.t(F32, 2048 * NSL, "scb")
            PTS = A.t(BF16, 2048 * 2, "pts")
            ASM = A.t(F32, 16 * NSL, "asm")

            NITER = NT * NH

            def at_meta(it):
                i, h = it // NH, it % NH
                return i, h, (it % NSL) * 2048, it % NSL

            def at_A(it):
                i, h, scv_off, sl = at_meta(it)
                c = h // 2
                pb = (h % 2) * 64
                L = (i + 1) * 128
                nblk = (L + 511) // 512
                for b5 in range(nblk):
                    k0 = b5 * 512
                    wdt = min(512, L - k0)
                    b = (0, 1, 7)[(it * 4 + b5) % 3]
                    ktk = [("kt", c, tbb) for tbb in
                           range(k0 // 512, (k0 + wdt - 1) // 512 + 1)]
                    mm(PS[b][:, 0:wdt],
                       QT[pb:pb + 64, c * 2048 + i * 128: c * 2048 + (i + 1) * 128],
                       KT[pb:pb + 64, c * 2048 + k0: c * 2048 + k0 + wdt],
                       True, True, r=[("qt", c, i // 4)] + ktk, w=[("ps", b)])
                    S.op("scalar", lambda e, o=SC[:, scv_off + k0: scv_off + k0 + wdt],
                         i2=PS[b][:, 0:wdt]: e.copy(o, i2),
                         r=[("ps", b)], w=[("sc", sl)])

            def at_B(it):
                i, h, scv_off, sl = at_meta(it)
                L = (i + 1) * 128
                vtt(ALU.add, SC[:, scv_off + L - 128: scv_off + L],
                    SC[:, scv_off + L - 128: scv_off + L], TRI[:, :],
                    r=[("sc", sl), ("tri",)], w=[("sc", sl)])
                rmax(ASM[:, sl * 8: sl * 8 + 1], SC[:, scv_off: scv_off + L],
                     r=[("sc", sl)], w=[("m1", sl)])
                vts(ASM[:, sl * 8 + 1: sl * 8 + 2], ASM[:, sl * 8: sl * 8 + 1],
                    -0.125, None, ALU.mult, r=[("m1", sl)], w=[("nm", sl)])

            def at_C(it):
                i, h, scv_off, sl = at_meta(it)
                L = (i + 1) * 128
                act(SC[:, scv_off: scv_off + L], SC[:, scv_off: scv_off + L],
                    AF.Exp, bias=ASM[:, sl * 8 + 1: sl * 8 + 2], scale=0.125,
                    accum=ASM[:, sl * 8 + 2: sl * 8 + 3],
                    r=[("sc", sl), ("nm", sl)], w=[("sc", sl), ("rs", sl)])

            def at_D(it):
                i, h, scv_off, sl = at_meta(it)
                c = h // 2
                pb = (h % 2) * 64
                L = (i + 1) * 128
                psl = it % 2
                vrec(ASM[:, sl * 8 + 3: sl * 8 + 4], ASM[:, sl * 8 + 2: sl * 8 + 3],
                     r=[("rs", sl)], w=[("ri", sl)])
                vts(SC[:, scv_off: scv_off + L], SC[:, scv_off: scv_off + L],
                    ASM[:, sl * 8 + 3: sl * 8 + 4], None, ALU.mult,
                    r=[("sc", sl), ("ri", sl)], w=[("sc", sl)])
                ngrp = (i + 1 + 3) // 4
                for g4 in range(ngrp):
                    b = (2, 3, 6)[(it * 4 + g4) % 3]
                    nk = min(4, i + 1 - g4 * 4)
                    for q4 in range(nk):
                        kc = g4 * 4 + q4
                        trans(PS[b][:, q4 * 128:(q4 + 1) * 128],
                              SC[:, scv_off + kc * 128: scv_off + (kc + 1) * 128],
                              IDF[:, :], r=[("sc", sl), ("idf",)], w=[("ps", b)])
                    dst = PTS[:, psl * 2048 + g4 * 512: psl * 2048 + g4 * 512 + nk * 128]
                    srcp = PS[b][:, 0: nk * 128]
                    if (it * 4 + g4) % 3 == 2:
                        S.op("scalar", lambda e, o=dst, i2=srcp: e.copy(o, i2),
                             r=[("ps", b)], w=[("pts", psl, g4)])
                    else:
                        vcp(dst, srcp, r=[("ps", b)], w=[("pts", psl, g4)])
                bao = 4 + (h % 2)
                for kc in range(i + 1):
                    mm(PS[bao][pb:pb + 64, 0:128],
                       VTOK[:, kc * 1024 + h * 64: kc * 1024 + h * 64 + 64],
                       PTS[:, psl * 2048 + kc * 128: psl * 2048 + (kc + 1) * 128],
                       kc == 0, kc == i, tp=(0, pb),
                       r=[("vtok", kc, h // 8), ("pts", psl, kc // 4)], w=[("ps", bao)])
                vcp(AOT[pb:pb + 64, c * 2048 + i * 128: c * 2048 + (i + 1) * 128],
                    PS[bao][pb:pb + 64, 0:128], r=[("ps", bao)],
                    w=[("aot", c, i, h % 2)])

            for t in range(NITER + 3):
                if t < NITER:
                    at_A(t)
                if 1 <= t < NITER + 1:
                    at_B(t - 1)
                if 2 <= t < NITER + 2:
                    at_C(t - 2)
                if 3 <= t < NITER + 3:
                    at_D(t - 3)

            if DEBUG_SPILLS and s == 1:
                dma("sync", d_dbgao, AOT[:, 0:16384],
                    r=[("aot", c, i, p) for c in range(NCH) for i in range(NT)
                       for p in range(2)],
                    out_sem=sOUT)
                n_out += 1
            S.fence()
            A.release()
            A.release()

            A.mark()
            WSTR2 = A.t(BF16, 8192, "wstr2")
            XNR = A.t(F32, 2048, "xnr")
            HSTG = A.t(F32, 2048, "hstg")
            for cc in range(NCH):
                dma("gpsimd", WSTR2[:, cc * 1024:(cc + 1) * 1024],
                    d_wot[cc * 128:(cc + 1) * 128, :], w=[("wo", cc)])
            for j in range(NT):
                sl = j % 2
                dma("sync", XNR[:, sl * 1024:(sl + 1) * 1024],
                    d_xnsp[j * 128:(j + 1) * 128, :],
                    r=[("xnsp", j)], w=[("xnr", sl)])
                for blk in range(2):
                    b = (j * 2 + blk) % 2
                    for cc in range(NCH):
                        mm(PS[b][:, 0:512],
                           AOT[:, cc * 2048 + j * 128: cc * 2048 + (j + 1) * 128],
                           WSTR2[:, cc * 1024 + blk * 512: cc * 1024 + (blk + 1) * 512],
                           cc == 0, cc == 7,
                           r=[("aot", cc, j, 0), ("aot", cc, j, 1), ("wo", cc)],
                           w=[("ps", b)])
                    vtt(ALU.add, HSTG[:, sl * 1024 + blk * 512: sl * 1024 + (blk + 1) * 512],
                        PS[b][:, 0:512], XNR[:, sl * 1024 + blk * 512: sl * 1024 + (blk + 1) * 512],
                        r=[("ps", b), ("xnr", sl)], w=[("hstg", sl, blk)])
                dma("scalar", d_hsp[j * 128:(j + 1) * 128, :],
                    HSTG[:, sl * 1024:(sl + 1) * 1024],
                    r=[("hstg", sl, 0), ("hstg", sl, 1)], w=[("hsp", j)])
            S.fence()
            A.release()
            A.release()

            A.mark()
            MOEW = A.t(BF16, D, "moew")
            A.mark()
            GROW2 = A.t(F32, D, "grow2")
            dma("sync", GROW2[0:1, :], d_moew, w=[("grow2",)])
            for nb in range(2):
                mm(PS[0][:, 0:512], ONESF[0:1, :],
                   GROW2[0:1, nb * 512:(nb + 1) * 512], True, True,
                   r=[("onesf",), ("grow2",)], w=[("ps", 0)])
                vcp(MOEW[:, nb * 512:(nb + 1) * 512], PS[0][:, 0:512],
                    r=[("ps", 0)], w=[("moeww",)])
            S.fence()
            A.release()

            HID = A.t(BF16, 32768, "hid")
            TNT = A.t(BF16, 8192, "tnt")
            W1B = A.t(BF16, 4096, "w1b")
            W2B = A.t(BF16, 2048, "w2b")
            B2E = A.t(BF16, 1024, "b2e")
            MOE = A.t(F32, 8192, "moe")
            HTS = A.t(F32, 2048, "hts")
            TNF = A.t(F32, 2048, "tnf")
            LOGW = A.t(F32, 128, "logw")
            GW = A.t(F32, 64, "gw")
            MSM = A.t(F32, 32, "msm")
            SST2 = A.t(F32, 2048, "sst2")
            XST2 = A.t(F32, 2048, "xst2")
            OUTS = A.t(F32, 2048, "outs")

            for tb in range(2):
                j0 = tb * 8
                vmemset(MOE[:, 0:8192], 0.0,
                        w=[("moe", jj, g) for jj in range(8) for g in range(2)])
                def t_T0(jj):
                    j = j0 + jj
                    sl = jj % 2
                    mb = sl * 16
                    lb = sl * 64
                    hv = HTS[:, sl * 1024:(sl + 1) * 1024]
                    dma("sync", hv, d_hsp[j * 128:(j + 1) * 128, :],
                        r=[("hsp", j)], w=[("hts", sl)])
                    vbn(LOGW[:, lb + 40: lb + 46], HTS[:, sl * 1024: sl * 1024 + 512],
                        r=[("hts", sl)], w=[("bn3", sl)])
                    vbn(LOGW[:, lb + 46: lb + 52], HTS[:, sl * 1024 + 512: (sl + 1) * 1024],
                        r=[("hts", sl)], w=[("bn3", sl)])
                    vbnagg(MSM[:, mb: mb + 2], LOGW[:, lb + 40: lb + 52],
                           r=[("bn3", sl)], w=[("mv3", sl)])
                    vtt(ALU.mult, MSM[:, mb + 2: mb + 3], MSM[:, mb: mb + 1],
                        MSM[:, mb: mb + 1], r=[("mv3", sl)], w=[("m3a", sl)])
                    vtt(ALU.add, MSM[:, mb + 3: mb + 4], MSM[:, mb + 2: mb + 3],
                        MSM[:, mb + 1: mb + 2], r=[("m3a", sl), ("mv3", sl)], w=[("m3b", sl)])
                    vts(MSM[:, mb + 4: mb + 5], MSM[:, mb + 3: mb + 4], 1e-6, None,
                        ALU.add, r=[("m3b", sl)], w=[("m3ba", sl)])
                    act(MSM[:, mb + 5: mb + 6], MSM[:, mb + 4: mb + 5], AF.Sqrt,
                        r=[("m3ba", sl)], w=[("m3c", sl)])
                    vrec(MSM[:, mb + 6: mb + 7], MSM[:, mb + 5: mb + 6],
                         r=[("m3c", sl)], w=[("m3d", sl)])

                def t_T1(jj):
                    sl = jj % 2
                    mb = sl * 16
                    hv = HTS[:, sl * 1024:(sl + 1) * 1024]
                    tnfv = TNF[:, sl * 1024:(sl + 1) * 1024]
                    vstt(tnfv, hv, MSM[:, mb + 6: mb + 7], MOEW[:, 0:1024],
                         ALU.mult, ALU.mult,
                         r=[("hts", sl), ("m3d", sl), ("moeww",)], w=[("tnf", sl)])
                    for cc in range(NCH):
                        b = 2 + cc % 2
                        trans(PS[b][:, 0:128], TNF[:, sl * 1024 + cc * 128: sl * 1024 + (cc + 1) * 128],
                              IDF[:, :], r=[("tnf", sl), ("idf",)], w=[("ps", b)])
                        vcp(TNT[:, cc * 1024 + jj * 128: cc * 1024 + (jj + 1) * 128],
                            PS[b][:, 0:128], r=[("ps", b)], w=[("tnt", cc, jj)])

                def t_T2(jj):
                    sl = jj % 2
                    mb = sl * 16
                    lb = sl * 64
                    bg = jj % 2
                    for cc in range(NCH):
                        mm(PS[bg][:, 0:8],
                           TNT[:, cc * 1024 + jj * 128: cc * 1024 + (jj + 1) * 128],
                           GATET[:, cc * 8:(cc + 1) * 8], cc == 0, cc == 7,
                           r=[("tnt", cc, jj), ("gatet",)], w=[("ps", bg)])
                    vcp(LOGW[:, lb: lb + 8], PS[bg][:, 0:8], r=[("ps", bg)], w=[("log", sl)])
                    rmax(MSM[:, mb + 8: mb + 9], LOGW[:, lb: lb + 8],
                         r=[("log", sl)], w=[("g1", sl)])
                    vts(LOGW[:, lb + 8: lb + 16], LOGW[:, lb: lb + 8],
                        MSM[:, mb + 8: mb + 9], None, ALU.is_equal,
                        r=[("log", sl), ("g1", sl)], w=[("mk1", sl)])
                    vstt(LOGW[:, lb + 16: lb + 24], LOGW[:, lb + 8: lb + 16], -1e30,
                         LOGW[:, lb: lb + 8], ALU.mult, ALU.add,
                         r=[("mk1", sl), ("log", sl)], w=[("l2", sl)])
                    rmax(MSM[:, mb + 9: mb + 10], LOGW[:, lb + 16: lb + 24],
                         r=[("l2", sl)], w=[("g2", sl)])
                    vts(LOGW[:, lb + 24: lb + 32], LOGW[:, lb + 16: lb + 24],
                        MSM[:, mb + 9: mb + 10], None, ALU.is_equal,
                        r=[("l2", sl), ("g2", sl)], w=[("mk2", sl)])
                    vts(MSM[:, mb + 10: mb + 11], MSM[:, mb + 8: mb + 9], -1.0, None,
                        ALU.mult, r=[("g1", sl)], w=[("ng1", sl)])
                    act(MSM[:, mb + 11: mb + 12], MSM[:, mb + 9: mb + 10], AF.Exp,
                        bias=MSM[:, mb + 10: mb + 11],
                        r=[("g2", sl), ("ng1", sl)], w=[("e2", sl)])
                    vts(MSM[:, mb + 12: mb + 13], MSM[:, mb + 11: mb + 12], 1.0, None,
                        ALU.add, r=[("e2", sl)], w=[("den", sl)])
                    vrec(MSM[:, mb + 13: mb + 14], MSM[:, mb + 12: mb + 13],
                         r=[("den", sl)], w=[("rd", sl)])
                    vstt(LOGW[:, lb + 32: lb + 40], LOGW[:, lb + 24: lb + 32],
                         MSM[:, mb + 11: mb + 12], LOGW[:, lb + 8: lb + 16],
                         ALU.mult, ALU.add,
                         r=[("mk2", sl), ("e2", sl), ("mk1", sl)], w=[("wn", sl)])
                    vts(GW[:, jj * 8:(jj + 1) * 8], LOGW[:, lb + 32: lb + 40],
                        MSM[:, mb + 13: mb + 14], None, ALU.mult,
                        r=[("wn", sl), ("rd", sl)], w=[("gw", jj)])

                for t_ in range(10):
                    if t_ < 8:
                        t_T0(t_)
                    if 1 <= t_ < 9:
                        t_T1(t_ - 1)
                    if 2 <= t_:
                        t_T2(t_ - 2)

                for e_ in range(NE):
                    for pr in range(16):
                        sb = pr % 2
                        r0 = (e_ * 32 + pr * 2) * 128
                        dma("gpsimd",
                            W1B[:, sb * 2048:(sb + 1) * 2048].rearrange(
                                "p (two d) -> p two d", two=2),
                            d_w1d[r0:r0 + 256, :].rearrange(
                                "(two p) d -> p two d", p=128),
                            w=[("w1b", sb)])
                        for sub in range(2):
                            b_ = pr * 2 + sub
                            for tsub in range(2):
                                hb = (b_ * 2 + tsub) % 4
                                for cc in range(NCH):
                                    mm(PS[hb][:, 0:512],
                                       W1B[:, sb * 2048 + sub * 1024 + cc * 128:
                                           sb * 2048 + sub * 1024 + (cc + 1) * 128],
                                       TNT[:, cc * 1024 + tsub * 512: cc * 1024 + (tsub + 1) * 512],
                                       cc == 0, cc == 7,
                                       r=[("w1b", sb)] + [("tnt", cc, t4)
                                                          for t4 in range(tsub * 4, tsub * 4 + 4)],
                                       w=[("ps", hb)])
                                act(HID[:, b_ * 1024 + tsub * 512: b_ * 1024 + (tsub + 1) * 512],
                                    PS[hb][:, 0:512], AF.Gelu,
                                    bias=B1T[:, e_ * 32 + b_: e_ * 32 + b_ + 1],
                                    r=[("ps", hb), ("b1t",)], w=[("hid", b_, tsub)])
                    dma("gpsimd", B2E[0:1, 0:1024], d_b2[e_:e_ + 1, :], w=[("b2e",)])
                    for g in range(2):
                        for kp in range(16):
                            sw = kp % 2
                            r0 = (((e_ * 2 + g) * 32 + kp * 2)) * 128
                            dma("gpsimd",
                                W2B[:, sw * 1024:(sw + 1) * 1024].rearrange(
                                    "p (two d) -> p two d", two=2),
                                d_w2d[r0:r0 + 256, :].rearrange(
                                    "(two p) d -> p two d", p=128),
                                w=[("w2b", sw)])
                            for sub in range(2):
                                k = kp * 2 + sub
                                for jj in range(8):
                                    mm(PS[jj][:, 0:512],
                                       HID[:, k * 1024 + jj * 128: k * 1024 + (jj + 1) * 128],
                                       W2B[:, sw * 1024 + sub * 512: sw * 1024 + (sub + 1) * 512],
                                       k == 0, False,
                                       r=[("hid", k, jj // 4), ("w2b", sw)],
                                       w=[("ps", jj)])
                        for jj in range(8):
                            mm(PS[jj][:, 0:512], ONESB[0:1, 0:128],
                               B2E[0:1, g * 512:(g + 1) * 512], False, True,
                               r=[("onesb",), ("b2e",)], w=[("ps", jj)])
                            vstt(MOE[:, jj * 1024 + g * 512: jj * 1024 + (g + 1) * 512],
                                 PS[jj][:, 0:512], GW[:, jj * 8 + e_: jj * 8 + e_ + 1],
                                 MOE[:, jj * 1024 + g * 512: jj * 1024 + (g + 1) * 512],
                                 ALU.mult, ALU.add,
                                 r=[("ps", jj), ("gw", jj), ("moe", jj, g)],
                                 w=[("moe", jj, g)])

                if DEBUG_SPILLS and s == 1 and tb == 1:
                    dma("sync", d_dbgtn, TNT[:, 0:8192],
                        r=[("tnt", cc, jj) for cc in range(NCH) for jj in range(8)],
                        out_sem=sOUT)
                    dma("sync", d_dbggw, GW[:, 0:64],
                        r=[("gw", jj) for jj in range(8)], out_sem=sOUT)
                    for jj in range(8):
                        dma("sync", d_dbgmoe[jj * 128:(jj + 1) * 128, :],
                            MOE[:, jj * 1024:(jj + 1) * 1024],
                            r=[("moe", jj, 0), ("moe", jj, 1)], out_sem=sOUT)
                    dma("sync", d_dbghid, HID[:, 0:32768],
                        r=[("hid", b_, t_) for b_ in range(32) for t_ in range(2)],
                        out_sem=sOUT)
                    n_out += 11

                def c_A(jj):
                    j = j0 + jj
                    sl = jj % 2
                    hv = HTS[:, sl * 1024:(sl + 1) * 1024]
                    dma("sync", hv, d_hsp[j * 128:(j + 1) * 128, :],
                        r=[("hsp", j)], w=[("hts", sl)])
                    fv = MOE[:, jj * 1024:(jj + 1) * 1024]
                    vtt(ALU.add, fv, hv, fv,
                        r=[("hts", sl), ("moe", jj, 0), ("moe", jj, 1)],
                        w=[("moe", jj, 0), ("moe", jj, 1)])
                    act(fv, fv, AF.Silu, r=[("moe", jj, 0)],
                        w=[("moe", jj, 0), ("moe", jj, 1)])
                    sv = SST2[:, sl * 1024:(sl + 1) * 1024]
                    dma("sync", sv, d_ssp[j * 128:(j + 1) * 128, :],
                        r=[("ssp", j)], w=[("sstr", sl)])
                    xv = XST2[:, sl * 1024:(sl + 1) * 1024]
                    dma("sync", xv, d_x[s * SEQ + j * 128: s * SEQ + (j + 1) * 128, :],
                        w=[("xstr", sl)])

                def c_B(jj):
                    j = j0 + jj
                    sl = jj % 2
                    fv = MOE[:, jj * 1024:(jj + 1) * 1024]
                    sv = SST2[:, sl * 1024:(sl + 1) * 1024]
                    xv = XST2[:, sl * 1024:(sl + 1) * 1024]
                    vtt(ALU.mult, fv, fv, sv,
                        r=[("moe", jj, 0), ("sstr", sl)],
                        w=[("moe", jj, 0), ("moe", jj, 1)])
                    ov = OUTS[:, sl * 1024:(sl + 1) * 1024]
                    vtt(ALU.add, ov, fv, xv,
                        r=[("moe", jj, 0), ("xstr", sl)], w=[("outs", sl)])
                    dma("sync", d_out[s * SEQ + j * 128: s * SEQ + (j + 1) * 128, :],
                        ov, r=[("outs", sl)], out_sem=sOUT)

                for t_ in range(9):
                    if t_ < 8:
                        c_A(t_)
                    if t_ >= 1:
                        c_B(t_ - 1)
                n_out += 8
            S.fence()
            A.release()

        stats = S.finalize_and_emit(block, final_waits=[(sOUT, n_out * 16)])
        total_ops = sum(v[0] for v in stats.values())
        print(f"[kernel] ops={stats} total={total_ops} arena_peak={A.peak}")
    return nc



def _host_tables():
    identf = np.eye(128, dtype=np.float32)
    tri = np.triu(np.full((128, 128), -1e30, np.float32), k=1)
    onesf = np.ones((1, 128), np.float32)
    onesb = np.ones((1, 128), BF)
    neg2 = np.concatenate([np.full((1, 16), -2 * LR, np.float32),
                           np.full((1, 16), -LR, np.float32)], axis=1)
    R = np.zeros((64, 64), np.float32)
    for i in range(32):
        R[i, i + 32] = -1.0
        R[i + 32, i] = 1.0
    R2 = np.zeros((128, 128), np.float32)
    R2[0:64, 0:64] = R
    R2[64:128, 64:128] = R
    r2t = R2.T.astype(BF)
    inv = (1.0 / (10000.0 ** (np.arange(0, 64, 2, dtype=np.float32) / 64.0)))
    ang = np.arange(SEQ, dtype=np.float32)[None, :] * inv[:, None]
    cos64 = np.concatenate([np.cos(ang), np.cos(ang)], axis=0)
    sin64 = np.concatenate([np.sin(ang), np.sin(ang)], axis=0)
    cosb = np.concatenate([cos64, cos64], axis=0).astype(BF)
    sinb = np.concatenate([sin64, sin64], axis=0).astype(BF)
    return identf, tri, onesf, onesb, neg2, r2t, cosb, sinb


def kernel(**inputs):
    x = np.asarray(inputs["L0_attn_x"], np.float32)
    Wb = np.ascontiguousarray(np.asarray(inputs["L0_attn_Wb"], np.float32))
    iA = np.asarray(inputs["L0_attn_init_A"], np.float32)
    iB = np.asarray(inputs["L0_attn_init_B"], np.float32)
    lng = np.asarray(inputs["L0_attn_ln_gamma"], np.float32)
    lnb = np.asarray(inputs["L0_attn_ln_beta"], np.float32)
    Wp = np.ascontiguousarray(np.asarray(inputs["L0_attn_Wp"], np.float32))
    anw = np.asarray(inputs["L0_ffn_attn_norm_w"], np.float32)
    wq = np.asarray(inputs["L0_ffn_wq"], np.float32)
    wk = np.asarray(inputs["L0_ffn_wk"], np.float32)
    wv = np.asarray(inputs["L0_ffn_wv"], np.float32)
    wo = np.asarray(inputs["L0_ffn_wo"], np.float32)
    mnw = np.asarray(inputs["L0_ffn_moe_norm_w"], np.float32)
    gw = np.asarray(inputs["L0_ffn_gate_w"], np.float32)
    w1 = np.asarray(inputs["L0_ffn_w1"], np.float32)
    b1 = np.asarray(inputs["L0_ffn_b1"], np.float32)
    w2 = np.asarray(inputs["L0_ffn_w2"], np.float32)
    b2 = np.asarray(inputs["L0_ffn_b2"], np.float32)

    identf, tri, onesf, onesb, neg2, r2t, cosb, sinb = _host_tables()

    w1d = np.ascontiguousarray(
        w1.reshape(8, 32, 128, 8, 128).transpose(0, 1, 4, 3, 2)).reshape(8 * 32 * 128, 1024)
    w2d = np.ascontiguousarray(
        w2.reshape(8, 2, 512, 32, 128).transpose(0, 1, 3, 4, 2)).reshape(8 * 2 * 32 * 128, 512)
    b1t = np.ascontiguousarray(b1.reshape(8, 32, 128).transpose(2, 0, 1)).reshape(128, 256)

    common = {
        "wbt": np.ascontiguousarray(Wb.T), "wpt": np.ascontiguousarray(Wp.T),
        "wpraw": Wp,
        "wqt": np.ascontiguousarray(wq.T), "wkt": np.ascontiguousarray(wk.T),
        "wvt": np.ascontiguousarray(wv.T), "wot": np.ascontiguousarray(wo.T),
        "w1d": w1d, "w2d": w2d, "b1t": b1t, "b2": np.ascontiguousarray(b2),
        "gatet": np.ascontiguousarray(gw.T),
        "initac": np.ascontiguousarray(iA),
        "initbtc": np.ascontiguousarray(iB.T),
        "initat": np.ascontiguousarray(iA.T),
        "initb": np.ascontiguousarray(iB),
        "lng": lng.reshape(1, D), "lnb": lnb.reshape(1, D),
        "moew": mnw.reshape(1, D),
        "attnwc": np.ascontiguousarray(anw.reshape(8, 128).T),
        "identf": identf, "tri": tri, "onesf": onesf, "onesb": onesb,
        "neg2": neg2, "r2t": r2t, "cosb": cosb, "sinb": sinb,
    }
    in_maps = []
    for core in range(8):
        m = dict(common)
        m["x"] = np.ascontiguousarray(x[2 * core:2 * core + 2].reshape(2 * SEQ, D))
        in_maps.append(m)

    nc = build_program()
    res = run_bass_kernel_spmd(nc, in_maps, core_ids=list(range(8)))
    outs = [np.asarray(r["out"], np.float32).reshape(2, SEQ, D) for r in res.results]
    return np.concatenate(outs, axis=0)
